# version 21
# baseline (speedup 1.0000x reference)
import numpy as np
import concourse.bass as bass
import concourse.mybir as mybir
from concourse.bass_utils import run_bass_kernel_spmd

F32 = mybir.dt.float32
BF16 = mybir.dt.bfloat16
AF = mybir.ActivationFunctionType
ALU = mybir.AluOpType
AX = mybir.AxisListType

S = 2048
E = 2048
H = 16
D = 128
C = 512
NT = 16
NCH = 4
TPC = 4
EPS = 1e-5
NEG = -1e30
NWO = 64
SCALE = 1.0 / float(np.sqrt(D))


class Sem:
    def __init__(self, nc, name):
        self.h = nc.alloc_semaphore(name)
        self.n = 0


class Prog:
    ENGINES = ("sync", "scalar", "vector", "gpsimd", "tensor")

    def __init__(self):
        self.ops = {e: [] for e in self.ENGINES}

    def emit(self, eng, fn):
        self.ops[eng].append(fn)

    def wait(self, eng, sem, val):
        if val is not None and val > 0:
            self.ops[eng].append(lambda e, s=sem.h, v=val: e.wait_ge(s, v))

    def drain(self, eng):
        self.ops[eng].append(lambda e: e.drain())


def build_nc(phases=("P1", "P2", "P3"), dbg=False):
    phases = set(phases)
    nc = bass.Bass("TRN2", target_bir_lowering=False)
    dbg_out = {}
    def dbg_t(name, shape, dtype):
        dbg_out[name] = nc.dram_tensor(name, shape, dtype, kind="ExternalOutput").ap()
        return dbg_out[name]

    x_in = nc.dram_tensor("x", [S, E], F32, kind="ExternalInput").ap()
    w_qkv = nc.dram_tensor("w_qkv", [3 * E, E], F32, kind="ExternalInput").ap()
    w_out = nc.dram_tensor("w_out", [E, E], F32, kind="ExternalInput").ap()
    W_ffn = nc.dram_tensor("W_ffn", [E, E], F32, kind="ExternalInput").ap()
    lnw_h = nc.dram_tensor("ln_w", [1, E], F32, kind="ExternalInput").ap()
    lnb_h = nc.dram_tensor("ln_b", [1, E], F32, kind="ExternalInput").ap()
    y_out = nc.dram_tensor("y", [S, E], F32, kind="ExternalOutput").ap()

    wT_scr = nc.dram_tensor("wT_scr", [16, 128, 16, 512], BF16, kind="Internal").ap()
    wq_scr = nc.dram_tensor("wq_scr", [E, E], BF16, kind="Internal").ap()
    xA_scr = nc.dram_tensor("xA_scr", [S, E], F32, kind="Internal").ap()
    xB_scr = nc.dram_tensor("xB_scr", [S, E], F32, kind="Internal").ap()

    ar1 = nc.alloc_sbuf_tensor("AR1", [128, 32768], F32).ap()
    ar2 = nc.alloc_sbuf_tensor("AR2", [128, 32768], BF16).ap()
    ar1b = ar1.bitcast(BF16)

    ident = nc.alloc_sbuf_tensor("ident", [128, 128], BF16).ap()
    identf = nc.alloc_sbuf_tensor("identf", [128, 128], F32).ap()
    maskU = nc.alloc_sbuf_tensor("maskU", [128, 128], F32).ap()
    maskL = nc.alloc_sbuf_tensor("maskL", [128, 128], F32).ap()
    zeros = nc.alloc_sbuf_tensor("zeros", [128, 128], F32).ap()
    onesq = nc.alloc_sbuf_tensor("onesq", [128, 128], F32).ap()
    onesp0 = nc.alloc_sbuf_tensor("onesp0", [1, 128], F32).ap()
    onescol = nc.alloc_sbuf_tensor("onescol", [128, 1], F32).ap()
    c_eps = nc.alloc_sbuf_tensor("c_eps", [128, 1], F32).ap()

    scpart = nc.alloc_sbuf_tensor("scpart", [128, 64], F32).ap()
    ln16t = nc.alloc_sbuf_tensor("ln16t", [16, 256], F32).ap()
    lnwcol = nc.alloc_sbuf_tensor("lnwcol", [128, 16], F32).ap()
    lnbcol = nc.alloc_sbuf_tensor("lnbcol", [128, 16], F32).ap()
    onesb = nc.alloc_sbuf_tensor("onesb", [1, 128], BF16).ap()
    c_eps2 = nc.alloc_sbuf_tensor("c_eps2", [128, 1], F32).ap()
    sccol = nc.alloc_sbuf_tensor("sccol", [128, 1], F32).ap()
    s11 = nc.alloc_sbuf_tensor("s11", [1, 1], F32).ap()
    thr = nc.alloc_sbuf_tensor("thr", [128, 1], F32).ap()
    nthr = nc.alloc_sbuf_tensor("nthr", [128, 1], F32).ap()
    tmpcmp = nc.alloc_sbuf_tensor("tmpcmp", [128, 2048], F32).ap()
    biasrow = tmpcmp[0:1, :].bitcast(BF16)[:, 0:2048]

    stats = [nc.alloc_sbuf_tensor(f"stats{i}", [128, 24], F32).ap() for i in range(2)]
    mv1 = [nc.alloc_sbuf_tensor(f"mv1_{i}", [128, 2], F32).ap() for i in range(2)]
    mv2 = [nc.alloc_sbuf_tensor(f"mv2_{i}", [128, 2], F32).ap() for i in range(2)]
    r1 = [nc.alloc_sbuf_tensor(f"r1_{i}", [128, 1], F32).ap() for i in range(2)]
    r2 = [nc.alloc_sbuf_tensor(f"r2_{i}", [128, 1], F32).ap() for i in range(2)]
    lntmp = [nc.alloc_sbuf_tensor(f"lntmp{i}", [128, 1], F32).ap() for i in range(2)]
    negmax = [nc.alloc_sbuf_tensor(f"negmax{i}", [128, 1], F32).ap() for i in range(2)]
    den = [nc.alloc_sbuf_tensor(f"den{i}", [128, 1], F32).ap() for i in range(2)]
    rden = [nc.alloc_sbuf_tensor(f"rden{i}", [128, 1], F32).ap() for i in range(2)]

    mm = [nc.alloc_psum_tensor(f"mm{i}", [128, 512], F32).ap() for i in range(2)]
    sc = [nc.alloc_psum_tensor(f"sc{i}", [128, 1024], F32).ap() for i in range(2)]
    scb = [t.bitcast(BF16) for t in sc]
    pt = [nc.alloc_psum_tensor(f"pt{i}", [128, 1024], BF16).ap() for i in range(2)]

    XST = [ar1[:, i * E:(i + 1) * E] for i in range(3)]
    LNW = ar1[:, 4096:4096 + E]
    LNB = ar1[:, 4096 + E:8192]
    HT = [ar1[:, 8192 + i * E:8192 + (i + 1) * E] for i in range(2)]
    T2 = [ar1[:, 12288 + i * E:12288 + (i + 1) * E] for i in range(2)]
    lnrow = T2[0][0:1, :]
    lnbrow = T2[1][0:1, :]
    T3 = [ar1[:, 16384 + i * E:16384 + (i + 1) * E] for i in range(2)]
    XN2 = [ar1b[:, 40960 + i * E:40960 + (i + 1) * E] for i in range(2)]
    XNT1 = [ar1b[:, 45056 + i * E:45056 + (i + 1) * E] for i in range(2)]
    WNAT = [ar1[:, 24576 + i * E:24576 + (i + 1) * E] for i in range(4)]
    WNATB = [ar1b[:, 16384 + i * E:16384 + (i + 1) * E] for i in range(4)]
    WTR = [ar1b[:, 12288 + i * E:12288 + (i + 1) * E] for i in range(2)]
    WACC = [ar1b[:, 24576 + i * 8192:24576 + (i + 1) * 8192] for i in range(2)]
    W_SB = ar1

    XLN = [ar1[:, i * E:(i + 1) * E] for i in range(2)]
    XOP = [ar1[:, 4096 + i * E:4096 + (i + 1) * E] for i in range(2)]
    KTB = [ar1b[:, 16384 + i * 8192:16384 + (i + 1) * 8192] for i in range(2)]
    VTB = [ar1b[:, 32768 + i * 8192:32768 + (i + 1) * 8192] for i in range(2)]
    QT = ar1b[:, 49152:57344]
    XOPS = [ar1[:, 4096 + i * 512:4096 + (i + 1) * 512] for i in range(2)]
    XPRS = [ar1[:, 5120 + i * 512:5120 + (i + 1) * 512] for i in range(2)]
    OTA = [ar1b[:, 57344:61440], ar1b[:, 12288:16384]]
    OTB = ar1b[:, 61440:65536]

    WQ = ar2
    XNTC = ar2[:, 0:8192]
    WSTR = [ar2[:, 8192 + i * 8192:8192 + (i + 1) * 8192] for i in range(2)]
    PRB = [ar2[:, 24576 + i * 640:24576 + (i + 1) * 640] for i in range(3)]
    ATT = [ar2[:, 26496 + i * 640:26496 + (i + 1) * 640] for i in range(2)]
    HB = [ar2[:, 27776 + i * E:27776 + (i + 1) * E] for i in range(2)]

    P = Prog()
    s_dc = Sem(nc, "s_dc")
    s_ds = Sem(nc, "s_ds")
    s_dg = Sem(nc, "s_dg")
    s_t = Sem(nc, "s_t")
    s_v = Sem(nc, "s_v")
    s_s = Sem(nc, "s_s")
    s_g = Sem(nc, "s_g")
    QSEM = {"scalar": s_dc, "sync": s_ds, "gpsimd": s_dg}
    s_ln1 = Sem(nc, "s_ln1"); s_wld = Sem(nc, "s_wld"); s_wqs = Sem(nc, "s_wqs")
    s_wc = Sem(nc, "s_wc");   s_wsa = Sem(nc, "s_wsa"); s_xl = Sem(nc, "s_xl")
    s_xs = Sem(nc, "s_xs");   s_xln = Sem(nc, "s_xln"); s_wl = Sem(nc, "s_wl")
    s_xop = Sem(nc, "s_xop"); s_xpr = Sem(nc, "s_xpr"); s_wqr = Sem(nc, "s_wqr")
    ESEM = {"tensor": s_t, "vector": s_v, "scalar": s_s, "gpsimd": s_g}

    def dma(queue, out, in_, sem=None):
        qs = sem if sem is not None else QSEM[queue]
        qs.n += 16
        P.emit(queue, lambda e, o=out, i=in_, s=qs.h: e.dma_start(out=o, in_=i).then_inc(s, 16))
        return qs.n

    def op(eng, fn, inc=False):
        if inc:
            es = ESEM[eng]
            es.n += 1
            P.emit(eng, lambda e, f=fn, s=es.h: f(e).then_inc(s, 1))
            return es.n
        P.emit(eng, fn)
        return None

    guard = {}

    def pguard(name):
        if name in guard:
            sem, cap = guard[name]
            P.wait("tensor", sem, cap)

    def pread(name, eng, cap):
        guard[name] = (ESEM[eng], cap)

    st8 = {}

    op("gpsimd", lambda e: e.memset(zeros, 0.0))
    op("gpsimd", lambda e: e.memset(onesq, 1.0))
    op("gpsimd", lambda e: e.memset(onesp0, 1.0))
    op("gpsimd", lambda e: e.memset(onescol, 1.0))
    op("gpsimd", lambda e: e.memset(c_eps, EPS))
    op("gpsimd", lambda e: e.memset(c_eps2, EPS * EPS))
    op("gpsimd", lambda e: e.memset(onesb, 1.0))
    P.drain("gpsimd")
    op("gpsimd", lambda e: e.affine_select(out=ident, in_=onesq, pattern=[[1, 128]],
                                           compare_op=ALU.is_equal, fill=0.0, base=0, channel_multiplier=-1))
    op("gpsimd", lambda e: e.affine_select(out=identf, in_=onesq, pattern=[[1, 128]],
                                           compare_op=ALU.is_equal, fill=0.0, base=0, channel_multiplier=-1))
    op("gpsimd", lambda e: e.affine_select(out=maskU, in_=zeros, pattern=[[1, 128]],
                                           compare_op=ALU.is_ge, fill=NEG, base=0, channel_multiplier=-1))
    c_consts = op("gpsimd", lambda e: e.affine_select(out=maskL, in_=zeros, pattern=[[-1, 128]],
                                                      compare_op=ALU.is_ge, fill=NEG, base=0,
                                                      channel_multiplier=1), inc=True)

    ptf0 = pt[0].bitcast(F32)
    dma("scalar", ln16t[:, 0:128], lnw_h.rearrange("a (t c) -> t (a c)", c=128), sem=s_ln1)
    c_lnr = dma("scalar", ln16t[:, 128:256], lnb_h.rearrange("a (t c) -> t (a c)", c=128), sem=s_ln1)

    c_wld = [dma("scalar", W_SB[:, t * E:(t + 1) * E], W_ffn[128 * t:128 * (t + 1), :], sem=s_wld)
             for t in range(NT)]

    P.wait("tensor", s_ln1, c_lnr)
    P.wait("tensor", s_g, c_consts)
    op("tensor", lambda e: e.transpose(out=ptf0[:, 0:16], in_=ln16t[:, 0:128],
                                       identity=identf[0:16, 0:16]))
    c_lnt = op("tensor", lambda e: e.transpose(out=ptf0[:, 16:32], in_=ln16t[:, 128:256],
                                               identity=identf[0:16, 0:16]), inc=True)
    P.wait("scalar", s_t, c_lnt)
    op("scalar", lambda e: e.copy(out=lnwcol, in_=ptf0[:, 0:16]))
    c_lncols = op("scalar", lambda e: e.copy(out=lnbcol, in_=ptf0[:, 16:32]), inc=True)
    pread("pt0", "scalar", c_lncols)

    for t in range(NT):
        P.wait("vector", s_wld, c_wld[t])
        for j in range(4):
            op("vector", lambda e, t=t, j=j: e.tensor_reduce(
                out=scpart[:, 4 * t + j:4 * t + j + 1],
                in_=W_SB[:, t * E + 512 * j:t * E + 512 * (j + 1)],
                axis=AX.X, op=ALU.add, apply_absolute_value=True))
    P.drain("vector")
    c_scp = op("vector", lambda e: e.tensor_reduce(out=sccol, in_=scpart, axis=AX.X, op=ALU.add), inc=True)
    P.wait("tensor", s_v, c_scp)
    P.wait("tensor", s_g, c_consts)
    c_ps1 = op("tensor", lambda e: e.matmul(out=mm[0][0:1, 0:1], lhsT=sccol, rhs=onescol,
                                            start=True, stop=True), inc=True)
    P.wait("vector", s_t, c_ps1)
    op("vector", lambda e: e.tensor_copy(out=s11, in_=mm[0][0:1, 0:1]))
    P.drain("vector")
    c_s11 = op("vector", lambda e: e.tensor_scalar(out=s11, in0=s11, scalar1=0.5 / float(E * E),
                                                   scalar2=None, op0=ALU.mult, op1=ALU.bypass), inc=True)
    pread("mm0", "vector", c_s11)
    P.wait("tensor", s_v, c_s11)
    c_ps2 = op("tensor", lambda e: e.matmul(out=mm[1][:, 0:1], lhsT=onesp0, rhs=s11,
                                            start=True, stop=True), inc=True)
    P.wait("vector", s_t, c_ps2)
    op("vector", lambda e: e.tensor_copy(out=thr, in_=mm[1][:, 0:1]))
    P.drain("vector")
    c_thr = op("vector", lambda e: e.tensor_scalar(out=nthr, in0=thr, scalar1=-1.0, scalar2=None,
                                                   op0=ALU.mult, op1=ALU.bypass), inc=True)
    pread("mm1", "vector", c_thr)
    P.drain("vector")

    LNBREP = ar1b[:, 12288:14336]
    c_wq_tile = [None] * NT
    c_lnbrep_t = [None] * NT

    def emit_lnbrep(t):
        c_lnbrep_t[t] = op("vector", lambda e, t=t: e.tensor_scalar(
            out=LNBREP[:, t * 128:(t + 1) * 128], in0=onesq, scalar1=lnbcol[:, t:t + 1],
            scalar2=None, op0=ALU.mult, op1=ALU.bypass), inc=True)

    for t in range(NT):
        op("vector", lambda e, t=t: e.tensor_scalar(
            out=tmpcmp, in0=W_SB[:, t * E:(t + 1) * E], scalar1=thr,
            scalar2=None, op0=ALU.is_gt, op1=ALU.bypass))
        op("vector", lambda e, t=t: e.tensor_scalar(
            out=WQ[:, t * E:(t + 1) * E], in0=W_SB[:, t * E:(t + 1) * E], scalar1=nthr,
            scalar2=None, op0=ALU.is_lt, op1=ALU.bypass))
        P.drain("vector")
        c_wq_tile[t] = op("vector", lambda e, t=t: e.tensor_tensor(
            out=WQ[:, t * E:(t + 1) * E], in0=tmpcmp, in1=WQ[:, t * E:(t + 1) * E],
            op=ALU.subtract), inc=True)
        if t == 3:
            P.wait("vector", s_s, c_lncols)
            for tt in range(4):
                emit_lnbrep(tt)
        elif t > 3:
            emit_lnbrep(t)
    st8["xst_wait"] = c_wq_tile[2]

    BB = [mm[0], mm[1], sc[0][:, 0:512], sc[1][:, 0:512]]
    BBN = ["mm0", "mm1", "sc0", "sc1"]
    for nm in BBN:
        pguard(nm)
    c_bias_t = [None] * NT
    for t in range(NT):
        P.wait("tensor", s_v, c_wq_tile[t])
        P.wait("tensor", s_v, c_lnbrep_t[t])
        for oc in range(4):
            cap = op("tensor", lambda e, t=t, oc=oc: e.matmul(
                out=BB[oc], lhsT=LNBREP[:, t * 128:(t + 1) * 128],
                rhs=WQ[:, t * E + oc * 512:t * E + (oc + 1) * 512],
                start=(t == 0), stop=(t == 15)), inc=(oc == 3))
        c_bias_t[t] = cap

    c_wqs_tile = [None] * NT
    for t in range(NT - 1):
        P.wait("scalar", s_t, c_bias_t[t])
        c_wqs_tile[t] = op("scalar", lambda e, t=t: e.activation(
            out=WQ[:, t * E:(t + 1) * E], in_=WQ[:, t * E:(t + 1) * E],
            func=AF.Copy, scale=lnwcol[:, t:t + 1]), inc=True)
    P.wait("scalar", s_t, c_bias_t[NT - 1])
    for oc in range(4):
        ccap = op("scalar", lambda e, oc=oc: e.copy(out=biasrow[:, oc * 512:(oc + 1) * 512],
                                                    in_=BB[oc][0:1, :]), inc=True)
        pread(BBN[oc], "scalar", ccap)
    c_wqs_tile[NT - 1] = op("scalar", lambda e, t=NT - 1: e.activation(
        out=WQ[:, t * E:(t + 1) * E], in_=WQ[:, t * E:(t + 1) * E],
        func=AF.Copy, scale=lnwcol[:, t:t + 1]), inc=True)
    st8["bias_done"] = c_wqs_tile[NT - 1]
    c_wq_done = c_wqs_tile[-1]

    if "P1" not in phases:
        for t in range(NT):
            P.wait("scalar", s_s, c_wqs_tile[t])
            dma("scalar", wq_scr[128 * t:128 * (t + 1), :], WQ[:, t * E:(t + 1) * E], sem=s_wqs)
    if dbg:
        dbg_t("d_wq0", [128, 2048], BF16)
        dbg_t("d_bias", [1, 2048], BF16)
        dbg_t("d_lnw", [128, 16], F32)

        P.wait("scalar", s_s, c_wqs_tile[0])
        dma("scalar", dbg_out["d_wq0"], WQ[:, 0:2048])
        P.wait("scalar", s_s, st8["bias_done"])
        dma("scalar", dbg_out["d_bias"], biasrow)
        dma("scalar", dbg_out["d_lnw"], lnwcol)


    c_wcast = [None] * NWO
    c_wcvt = [None] * NWO
    c_wtr = [None] * NWO
    c_wtre = [None] * NWO
    c_gst = [None] * (NWO // 4)

    def w_src(j):
        if j < 48:
            return w_qkv[128 * j:128 * (j + 1), :]
        return w_out[128 * (j - 48):128 * (j - 47), :]

    def emit_wcast(j):
        if j >= 4:
            assert c_wcvt[j - 4] is not None
            P.wait("sync", s_v, c_wcvt[j - 4])
        else:
            P.wait("sync", s_v, c_wq_tile[-1])
        c_wcast[j] = dma("sync", WNAT[j % 4], w_src(j), sem=s_wc)
        P.wait("vector", s_wc, c_wcast[j])
        if j >= 4:
            P.wait("vector", s_t, c_wtr[j - 4])
        c_wcvt[j] = op("vector", lambda e, j=j: e.tensor_copy(out=WNATB[j % 4], in_=WNAT[j % 4]), inc=True)

    def emit_wtr(j):
        P.wait("tensor", s_v, c_wcvt[j])
        if j >= 2:
            P.wait("tensor", s_s, c_wtre[j - 2])
        else:
            pguard("sc0")
            pguard("sc1")
        cap = None
        for ib in range(16):
            cap = op("tensor", lambda e, j=j, ib=ib: e.transpose(
                out=scb[j % 2][:, ib * 128:(ib + 1) * 128],
                in_=WNATB[j % 4][:, ib * 128:(ib + 1) * 128], identity=ident), inc=(ib == 15))
        c_wtr[j] = cap

    def emit_wtre(j):
        g, u = j // 4, j % 4
        P.wait("scalar", s_t, c_wtr[j])
        if u == 0 and g >= 2:
            P.wait("scalar", s_wsa, c_gst[g - 2])
        dst = WACC[g % 2].rearrange("p (k o) -> p k o", o=512)[:, :, u * 128:(u + 1) * 128]
        src = scb[j % 2].rearrange("p (k o) -> p k o", o=128)
        c_wtre[j] = op("scalar", lambda e, dst=dst, src=src: e.copy(out=dst, in_=src), inc=True)
        pread("sc%d" % (j % 2), "scalar", c_wtre[j])


    def emit_wstore(g):
        P.wait("sync", s_s, c_wtre[4 * g + 3])
        if g < 12:
            dst = wT_scr[g]
        else:
            dst = wT_scr[g]
        c_gst[g] = dma("sync", dst.rearrange("p k o -> p (k o)"), WACC[g % 2], sem=s_wsa)

    done_p1 = [False]

    def ffn_stage(src_hbm, dst_hbm, with_wtr):
        c_load = [None] * NT
        c_stats1 = [None] * NT
        c_n1s2 = [None] * NT
        c_t2 = [None] * NT
        c_xn2 = [None] * NT
        c_transp = [None] * NT
        c_xnt = [None] * NT
        c_rstd1 = [None] * NT
        c_rstd2 = [None] * NT
        c_mmg = [[None] * 4 for _ in range(NT)]
        c_stt = [[None] * 4 for _ in range(NT)]
        c_store = [None] * NT

        def emit_xntr(i):
            P.wait("tensor", s_v, c_xn2[i])
            pguard("pt0")
            pguard("pt1")
            cap = None
            for ib in range(16):
                cap = op("tensor", lambda e, i=i, ib=ib: e.transpose(
                    out=pt[ib // 8][:, (ib % 8) * 128:(ib % 8 + 1) * 128],
                    in_=XN2[i % 2][:, ib * 128:(ib + 1) * 128], identity=ident), inc=(ib == 15))
            c_transp[i] = cap

        for i in range(NT + 2):
            if i < NT:
                if i < 3:
                    P.wait("scalar", s_v, st8["xst_wait"])
                else:
                    P.wait("scalar", s_xs, c_store[i - 3])
                c_load[i] = dma("scalar", XST[i % 3], src_hbm[128 * i:128 * (i + 1), :], sem=s_xl)
            if with_wtr and i < NT:
                P.wait("sync", s_s, c_wqs_tile[i])
                dma("sync", wq_scr[128 * i:128 * (i + 1), :], WQ[:, i * E:(i + 1) * E], sem=s_wqs)
            if i >= 2:
                P.wait("scalar", s_v, c_stt[i - 2][3])
                c_store[i - 2] = dma("scalar", dst_hbm[128 * (i - 2):128 * (i - 1), :],
                                     XST[(i - 2) % 3], sem=s_xs)

            if 1 <= i <= NT:
                ii = i - 1
                P.wait("vector", s_t, c_transp[ii])
                op("vector", lambda e, ii=ii: e.tensor_copy(out=XNT1[ii % 2][:, 0:1024], in_=pt[0]))
                c_xnt[ii] = op("vector", lambda e, ii=ii: e.tensor_copy(out=XNT1[ii % 2][:, 1024:2048],
                                                                        in_=pt[1]), inc=True)
                pread("pt0", "vector", c_xnt[ii])
                pread("pt1", "vector", c_xnt[ii])

            if i < NT:
                P.wait("vector", s_xl, c_load[i])

                def v_stats1(e, i=i):
                    for j in range(4):
                        e.bn_stats(stats[i % 2][:, 6 * j:6 * (j + 1)], XST[i % 3][:, 512 * j:512 * (j + 1)])
                    e.drain()
                    return e.bn_aggr(mv1[i % 2], stats[i % 2])
                c_stats1[i] = op("vector", v_stats1, inc=True)

                P.wait("scalar", s_v, c_stats1[i])
                op("scalar", lambda e, i=i: e.activation(out=lntmp[i % 2], in_=mv1[i % 2][:, 1:2],
                                                         func=AF.Ln, scale=1.0 + EPS, bias=c_eps2[:, 0:1]))
                P.drain("scalar")
                c_rstd1[i] = op("scalar", lambda e, i=i: e.activation(out=r1[i % 2], in_=lntmp[i % 2],
                                                                     func=AF.Exp, scale=-0.5), inc=True)

                P.wait("vector", s_s, c_rstd1[i])
                if i >= 2:
                    P.wait("vector", s_t, c_transp[i - 2])
                c_xn2[i] = op("vector", lambda e, i=i: e.tensor_scalar(
                    out=XN2[i % 2], in0=XST[i % 3], scalar1=mv1[i % 2][:, 0:1],
                    scalar2=r1[i % 2], op0=ALU.subtract, op1=ALU.mult), inc=True)

                if with_wtr and i >= 1:
                    for j in range(4 * (i - 1), 4 * (i - 1) + 4):
                        emit_wcast(j)


            if with_wtr and 0 <= i - 2 < NT:
                emit_wstore(i - 2)

            if 1 <= i <= NT:
                ii = i - 1
                P.wait("tensor", s_v, c_xnt[ii])
                if ii == 0:
                    P.wait("tensor", s_s, st8.get("bias_done", 0))
                    P.wait("tensor", s_wqr, st8.get("wq_reload", 0))
                if with_wtr and ii == NT - 1:
                    for j in range(4 * ii, 4 * ii + 4):
                        emit_wcast(j)
                def emit_stt(ii, oc):
                    if oc == 0 and ii >= 3:
                        P.wait("vector", s_xs, c_store[ii - 3])
                    P.wait("vector", s_t, c_mmg[ii][oc])
                    c_stt[ii][oc] = op("vector", lambda e, ii=ii, oc=oc: e.scalar_tensor_tensor(
                        out=XST[ii % 3][:, oc * 512:(oc + 1) * 512], in0=mm[oc % 2], scalar=0.5,
                        in1=XST[ii % 3][:, oc * 512:(oc + 1) * 512], op0=ALU.mult, op1=ALU.add), inc=True)
                    pread("mm%d" % (oc % 2), "vector", c_stt[ii][oc])

                for oc in range(4):
                    if with_wtr:
                        emit_wtr(4 * ii + oc)
                        emit_wtre(4 * ii + oc)
                    if oc == 2 and i < NT:
                        emit_xntr(i)
                    if oc >= 2:
                        emit_stt(ii, oc - 2)
                    pguard("mm%d" % (oc % 2))
                    op("tensor", lambda e, oc=oc: e.matmul(
                        out=mm[oc % 2], lhsT=onesb, rhs=biasrow[:, oc * 512:(oc + 1) * 512],
                        start=True, stop=False))
                    cap = None
                    for k in range(16):
                        cap = op("tensor", lambda e, ii=ii, oc=oc, k=k: e.matmul(
                            out=mm[oc % 2], lhsT=XNT1[ii % 2][:, k * 128:(k + 1) * 128],
                            rhs=WQ[:, k * E + oc * 512:k * E + (oc + 1) * 512],
                            start=False, stop=(k == 15)), inc=(k == 15))
                    c_mmg[ii][oc] = cap
                emit_stt(ii, 2)
                emit_stt(ii, 3)

            if i == 0:
                emit_xntr(0)


        done_p1[0] = True
        return {"store": c_store, "last_mm": c_mmg[NT - 1][3]}

    p1_meta = None
    if "P1" in phases:
        p1_dst = xA_scr if "P2" in phases else y_out
        p1_meta = ffn_stage(x_in, p1_dst, with_wtr=True)
    elif "P2" in phases:
        for j in range(NWO):
            emit_wcast(j)
            emit_wtr(j)
            emit_wtre(j)
            if j % 4 == 3:
                emit_wstore(j // 4)

    if "P2" in phases:
        p2_src = xA_scr if "P1" in phases else x_in
        p2_dst = xB_scr if "P3" in phases else y_out

        NTI = NCH * TPC
        c_xln = [None] * NTI
        c_h = [None] * NTI
        c_ltr = [None] * NTI
        c_xntc = [None] * NTI
        c_qkvg = [None] * (NCH * 48)
        c_qkve = [None] * (NCH * 48)
        NG = NCH * H * 4
        c_scores = [None] * NG
        c_negmax = [None] * NG
        c_mask = [None] * NG
        c_recip = [None] * NG
        c_exp = [None] * NG
        c_norm = [None] * NG
        c_trb = [None] * NG
        c_atte = [None] * NG
        c_pv = [None] * NG
        c_ote = [None] * (NCH * H)
        c_opg = [[None] * 16 for _ in range(NCH)]
        c_stt2 = [None] * NTI
        c_stt2f = [None] * (NCH * 16)
        c_xopf = [None] * (NCH * 16)
        c_xprf = [None] * (NCH * 16)

        def seg_list(n, t):
            if n == 0:
                return [(0, 0, 128 * (t + 1), 0)], 128 * (t + 1)
            segs = [(-1, 128 * t, 512 - 128 * t, 0)]
            if t > 0:
                segs.append((0, 0, 128 * t, 512 - 128 * t))
            segs.append((0, 128 * t, 128, 512))
            return segs, 640

        def pv_tiles(n, t):
            if n == 0:
                return [(0, j, 128 * j) for j in range(t + 1)]
            return [((-1 if t + j < 4 else 0), (t + j) % 4, 128 * j) for j in range(5)]

        def emit_ln(m):
            for st in range(TPC):
                gi = m * TPC + st
                if gi >= 2:
                    P.wait("scalar", s_v, c_h[gi - 2])
                c_xln[gi] = dma("scalar", XLN[gi % 2], p2_src[128 * gi:128 * (gi + 1), :], sem=s_xln)
                P.wait("vector", s_xln, c_xln[gi])

                def v_stats(e, gi=gi):
                    for j in range(4):
                        e.bn_stats(stats[gi % 2][:, 6 * j:6 * (j + 1)], XLN[gi % 2][:, 512 * j:512 * (j + 1)])
                    e.drain()
                    return e.bn_aggr(mv1[gi % 2], stats[gi % 2])
                cs = op("vector", v_stats, inc=True)
                P.wait("scalar", s_v, cs)
                op("scalar", lambda e, gi=gi: e.activation(out=lntmp[gi % 2], in_=mv1[gi % 2][:, 1:2],
                                                           func=AF.Ln, bias=c_eps[:, 0:1]))
                P.drain("scalar")
                cr = op("scalar", lambda e, gi=gi: e.activation(out=r1[gi % 2], in_=lntmp[gi % 2],
                                                                func=AF.Exp, scale=-0.5), inc=True)
                P.wait("vector", s_s, cr)
                if gi >= 2:
                    P.wait("vector", s_t, c_ltr[gi - 2])
                c_h[gi] = op("vector", lambda e, gi=gi: e.tensor_scalar(
                    out=HB[gi % 2], in0=XLN[gi % 2], scalar1=mv1[gi % 2][:, 0:1],
                    scalar2=r1[gi % 2], op0=ALU.subtract, op1=ALU.mult), inc=True)
                P.wait("tensor", s_v, c_h[gi])
                pguard("pt0")
                pguard("pt1")
                cap = None
                for ib in range(16):
                    cap = op("tensor", lambda e, gi=gi, ib=ib: e.transpose(
                        out=pt[ib // 8][:, (ib % 8) * 128:(ib % 8 + 1) * 128],
                        in_=HB[gi % 2][:, ib * 128:(ib + 1) * 128], identity=ident), inc=(ib == 15))
                c_ltr[gi] = cap
                P.wait("vector", s_t, c_ltr[gi])
                xv = XNTC.rearrange("p (k s) -> p k s", s=512)
                op("vector", lambda e, st=st, xv=xv: e.tensor_copy(
                    out=xv[:, 0:8, st * 128:(st + 1) * 128],
                    in_=pt[0].rearrange("p (k s) -> p k s", s=128)))
                c_xntc[gi] = op("vector", lambda e, st=st, xv=xv: e.tensor_copy(
                    out=xv[:, 8:16, st * 128:(st + 1) * 128],
                    in_=pt[1].rearrange("p (k s) -> p k s", s=128)), inc=True)
                pread("pt0", "vector", c_xntc[gi])
                pread("pt1", "vector", c_xntc[gi])

        wseq = {"pos": 0}
        c_wlp = [None] * (NCH * 16)
        wuse_p = [None] * (NCH * 16)

        def emit_wload(j):
            pos = wseq["pos"]
            wseq["pos"] += 1
            if pos >= 2:
                assert wuse_p[pos - 2] is not None, f"wload pos {pos} before consumer of {pos-2}"
                P.wait("sync", s_t, wuse_p[pos - 2])
            else:
                P.wait("sync", s_wsa, c_gst[NWO // 4 - 1])
                if p1_meta is not None:
                    P.wait("sync", s_t, p1_meta["last_mm"])
            src = wT_scr[j].rearrange("p k o -> p (k o)")
            c_wlp[pos] = dma("sync", WSTR[pos % 2], src, sem=s_wl)
            return pos

        def emit_qkv(n, qpos):
            for oc in range(12):
                u = qpos[oc]
                for grp in range(4):
                    g_idx = n * 48 + oc * 4 + grp
                    P.wait("tensor", s_wl, c_wlp[u])
                    if grp == 0 and oc == 0:
                        P.wait("tensor", s_v, c_xntc[n * TPC + 3])
                    pguard("mm%d" % (g_idx % 2))
                    bank = mm[g_idx % 2]
                    cap = None
                    if oc < 8:
                        for k in range(16):
                            cap = op("tensor", lambda e, u=u, grp=grp, k=k, bank=bank: e.matmul(
                                out=bank, lhsT=WSTR[u % 2][:, k * 512 + grp * 128:k * 512 + (grp + 1) * 128],
                                rhs=XNTC[:, k * 512:(k + 1) * 512],
                                start=(k == 0), stop=(k == 15)), inc=(k == 15))
                    else:
                        for k in range(16):
                            cap = op("tensor", lambda e, u=u, grp=grp, k=k, bank=bank: e.matmul(
                                out=bank, lhsT=XNTC[:, k * 512 + grp * 128:k * 512 + (grp + 1) * 128],
                                rhs=WSTR[u % 2][:, k * 512:(k + 1) * 512],
                                start=(k == 0), stop=(k == 15)), inc=(k == 15))
                    c_qkvg[g_idx] = cap
                    wuse_p[u] = cap
                    P.wait("scalar", s_t, cap)
                    if oc < 4:
                        h = oc * 4 + grp
                        dst = QT[:, h * 512:(h + 1) * 512]
                        c_qkve[g_idx] = op("scalar", lambda e, dst=dst, bank=bank: e.mul(
                            out=dst, in_=bank, mul=SCALE), inc=True)
                    else:
                        if oc < 8:
                            h = (oc - 4) * 4 + grp
                            dst = KTB[n % 2][:, h * 512:(h + 1) * 512]
                        else:
                            dst = VTB[n % 2][:, grp * 2048 + (oc - 8) * 512:grp * 2048 + (oc - 7) * 512]
                        c_qkve[g_idx] = op("scalar", lambda e, dst=dst, bank=bank: e.copy(
                            out=dst, in_=bank), inc=True)
                    pread("mm%d" % (g_idx % 2), "scalar", c_qkve[g_idx])
                if oc + 2 <= 11:
                    qpos.append(emit_wload(oc + 2))

        def emit_attn(n):
            g0 = n * H * 4
            blocks = [(h, t) for h in range(H) for t in range(4)]

            def width_of(t):
                return 128 * (t + 1) if n == 0 else 640

            def emit_scores(h, t, g):
                segs, width = seg_list(n, t)
                pguard("sc%d" % (g % 2))
                if h == 0 and t == 0:
                    P.wait("tensor", s_s, c_qkve[n * 48 + 47])
                cap = None
                for si, (pd, soff, w, doff) in enumerate(segs):
                    par = (n + pd) % 2
                    cap = op("tensor", lambda e, h=h, t=t, g=g, par=par, soff=soff, w=w, doff=doff: e.matmul(
                        out=sc[g % 2][:, doff:doff + w],
                        lhsT=QT[:, h * 512 + t * 128:h * 512 + (t + 1) * 128],
                        rhs=KTB[par][:, h * 512 + soff:h * 512 + soff + w],
                        start=True, stop=True), inc=(si == len(segs) - 1))
                c_scores[g] = cap

            def emit_vblock(h, t, g):
                width = width_of(t)
                P.wait("vector", s_t, c_scores[g])
                c_negmax[g] = op("vector", lambda e, g=g, width=width: e.tensor_reduce(
                    out=negmax[g % 2], in_=sc[g % 2][:, 0:width], axis=AX.X, op=ALU.max,
                    negate=True), inc=True)
                if n == 0:
                    c_mask[g] = op("vector", lambda e, t=t, g=g: e.tensor_tensor(
                        out=sc[g % 2][:, 128 * t:128 * (t + 1)], in0=sc[g % 2][:, 128 * t:128 * (t + 1)],
                        in1=maskL, op=ALU.add), inc=True)
                else:
                    op("vector", lambda e, g=g: e.tensor_tensor(
                        out=sc[g % 2][:, 0:128], in0=sc[g % 2][:, 0:128], in1=maskU, op=ALU.add))
                    c_mask[g] = op("vector", lambda e, g=g: e.tensor_tensor(
                        out=sc[g % 2][:, 512:640], in0=sc[g % 2][:, 512:640], in1=maskL, op=ALU.add), inc=True)

            def emit_exp(h, t, g):
                width = width_of(t)
                P.wait("scalar", s_v, c_mask[g])
                if g - 3 >= 0 and c_trb[g - 3] is not None:
                    P.wait("scalar", s_t, c_trb[g - 3])
                c_exp[g] = op("scalar", lambda e, g=g, width=width: e.activation(
                    out=PRB[g % 3][:, 0:width], in_=sc[g % 2][:, 0:width], func=AF.Exp,
                    bias=negmax[g % 2], scale=1.0, accum_out=den[g % 2]), inc=True)
                pread("sc%d" % (g % 2), "scalar", c_exp[g])

            def emit_recip(g):
                P.wait("vector", s_s, c_exp[g])
                c_recip[g] = op("vector", lambda e, g=g: e.reciprocal(rden[g % 2], den[g % 2]), inc=True)

            def emit_norm(g, t):
                width = width_of(t)
                P.wait("scalar", s_v, c_recip[g])
                c_norm[g] = op("scalar", lambda e, g=g, width=width: e.activation(
                    out=PRB[g % 3][:, 0:width], in_=PRB[g % 3][:, 0:width], func=AF.Copy,
                    scale=rden[g % 2]), inc=True)

            def emit_transp(g, t):
                width = width_of(t)
                P.wait("tensor", s_s, c_norm[g])
                pguard("pt%d" % (g % 2))
                nt_ = width // 128
                cap = None
                for j in range(nt_):
                    cap = op("tensor", lambda e, g=g, j=j: e.transpose(
                        out=pt[g % 2][:, j * 128:(j + 1) * 128],
                        in_=PRB[g % 3][:, j * 128:(j + 1) * 128], identity=ident), inc=(j == nt_ - 1))
                c_trb[g] = cap

            def emit_atte(g, t):
                width = width_of(t)
                P.wait("vector", s_t, c_trb[g])
                if g - 2 >= 0 and c_pv[g - 2] is not None:
                    P.wait("vector", s_t, c_pv[g - 2])
                c_atte[g] = op("vector", lambda e, g=g, width=width: e.tensor_copy(
                    out=ATT[g % 2][:, 0:width], in_=pt[g % 2][:, 0:width]), inc=True)
                pread("pt%d" % (g % 2), "vector", c_atte[g])

            def emit_pv(h, t, g):
                P.wait("tensor", s_v, c_atte[g])
                if t == 0:
                    pguard("mm%d" % (h % 2))
                tiles = pv_tiles(n, t)
                cap = None
                for j, (pd, stile, aoff) in enumerate(tiles):
                    cap = op("tensor", lambda e, h=h, t=t, g=g, pd=pd, stile=stile, aoff=aoff,
                             j=j, lastj=len(tiles) - 1: e.matmul(
                        out=mm[h % 2][:, t * 128:(t + 1) * 128],
                        lhsT=VTB[(n + pd) % 2][:, stile * 2048 + h * 128:stile * 2048 + (h + 1) * 128],
                        rhs=ATT[g % 2][:, aoff:aoff + 128],
                        start=(j == 0), stop=(j == lastj)), inc=(j == len(tiles) - 1))
                c_pv[g] = cap
                if t == 3:
                    hh = n * H + h
                    P.wait("scalar", s_t, c_pv[g])
                    if h < 8:
                        dst = OTA[n % 2][:, h * 512:(h + 1) * 512]
                    else:
                        dst = OTB[:, (h - 8) * 512:(h - 7) * 512]
                        if n >= 1 and h == 8:
                            P.wait("scalar", s_t, c_opg[n - 1][15])
                    c_ote[hh] = op("scalar", lambda e, dst=dst, h=h: e.copy(
                        out=dst, in_=mm[h % 2]), inc=True)
                    pread("mm%d" % (h % 2), "scalar", c_ote[hh])

            NB = len(blocks)
            m = n - 1
            op_state = {"next": 0, "upos": None, "nxt": []}
            if m >= 0:
                op_state["upos"] = emit_wload(12)
                op_state["nxt"].append(emit_wload(13))

            def inject_opgroups(k):
                for _ in range(k):
                    gq2 = op_state["next"]
                    if gq2 >= 16:
                        return
                    if gq2 % 4 == 0 and gq2 > 0:
                        op_state["upos"] = op_state["nxt"].pop(0)
                        if 12 + gq2 // 4 + 1 <= 15:
                            op_state["nxt"].append(emit_wload(12 + gq2 // 4 + 1))
                    emit_opgroup(m, gq2, op_state["upos"])
                    op_state["next"] = gq2 + 1

            for bi in range(NB + 3):
                if bi < NB:
                    h, t = blocks[bi]
                    emit_scores(h, t, g0 + bi)
                    emit_vblock(h, t, g0 + bi)
                    emit_exp(h, t, g0 + bi)
                if 0 <= bi - 1 < NB:
                    emit_recip(g0 + bi - 1)
                    emit_norm(g0 + bi - 1, blocks[bi - 1][1])
                if 0 <= bi - 2 < NB:
                    emit_transp(g0 + bi - 2, blocks[bi - 2][1])
                    emit_atte(g0 + bi - 2, blocks[bi - 2][1])
                if 0 <= bi - 3 < NB:
                    emit_pv(blocks[bi - 3][0], blocks[bi - 3][1], g0 + bi - 3)
                    if m >= 0 and blocks[bi - 3][1] == 3:
                        inject_opgroups(2)
            if m >= 0:
                inject_opgroups(16)

        def ot_lhsT(m, k, st):
            if k < 8:
                return OTA[m % 2][:, k * 512 + st * 128:k * 512 + (st + 1) * 128]
            return OTB[:, (k - 8) * 512 + st * 128:(k - 8) * 512 + (st + 1) * 128]

        def emit_opgroup(m, gq2, u):
            ec, st = gq2 // 4, gq2 % 4
            gg = m * 16 + gq2
            gi = m * TPC + st
            if gg >= 2:
                P.wait("scalar", s_v, c_stt2f[gg - 2])
            c_xopf[gg] = dma("scalar", XOPS[gq2 % 2],
                             p2_src[128 * gi:128 * (gi + 1), ec * 512:(ec + 1) * 512], sem=s_xop)
            P.wait("tensor", s_wl, c_wlp[u])
            pguard("mm%d" % (gq2 % 2))
            if gq2 == 0:
                P.wait("tensor", s_s, c_ote[m * H + 15])
            bank = mm[gq2 % 2]
            cap = None
            for k in range(16):
                cap = op("tensor", lambda e, m=m, u=u, st=st, k=k, bank=bank: e.matmul(
                    out=bank, lhsT=ot_lhsT(m, k, st),
                    rhs=WSTR[u % 2][:, k * 512:(k + 1) * 512],
                    start=(k == 0), stop=(k == 15)), inc=(k == 15))
            c_opg[m][gq2] = cap
            wuse_p[u] = cap
            P.wait("vector", s_t, cap)
            P.wait("vector", s_xop, c_xopf[gg])
            if gg >= 2:
                P.wait("vector", s_xpr, c_xprf[gg - 2])
            cstt = op("vector", lambda e, gq2=gq2, bank=bank: e.tensor_tensor(
                out=XPRS[gq2 % 2], in0=bank, in1=XOPS[gq2 % 2], op=ALU.add), inc=True)
            pread("mm%d" % (gq2 % 2), "vector", cstt)
            c_stt2f[gg] = cstt
            if ec == 3:
                c_stt2[gi] = cstt
            P.wait("scalar", s_v, cstt)
            c_xprf[gg] = dma("scalar",
                             p2_dst[128 * gi:128 * (gi + 1), ec * 512:(ec + 1) * 512],
                             XPRS[gq2 % 2], sem=s_xpr)

        def emit_oproj(n):
            upos = emit_wload(12)
            nxt = [emit_wload(13)]
            for gq2 in range(16):
                if gq2 % 4 == 0 and gq2 > 0:
                    upos = nxt.pop(0)
                    if 12 + gq2 // 4 + 1 <= 15:
                        nxt.append(emit_wload(12 + gq2 // 4 + 1))
                emit_opgroup(n, gq2, upos)

        G_DBG = 1
        if dbg:
            dbg_t("d_xnt", [128, 8192], BF16)
            dbg_t("d_q", [128, 8192], BF16)
            dbg_t("d_k", [128, 8192], BF16)
            dbg_t("d_v", [128, 8192], BF16)
            dbg_t("d_sc", [128, 256], F32)
            dbg_t("d_nm", [128, 1], F32)
            dbg_t("d_den", [128, 1], F32)
            dbg_t("d_prb", [128, 256], BF16)
            dbg_t("d_att", [128, 256], BF16)
            dbg_t("d_ot", [128, 8192], BF16)
            dbg_t("d_ws", [128, 8192], BF16)
            dbg_t("d_scr", [256, 512], BF16)

        emit_ln(0)
        for n in range(NCH):
            qpos = [emit_wload(0), emit_wload(1)]
            emit_qkv(n, qpos)
            if dbg and n == 0:
                P.wait("scalar", s_v, c_xntc[3])
                dma("scalar", dbg_out["d_xnt"], XNTC)
                P.wait("scalar", s_s, c_qkve[47])
                dma("scalar", dbg_out["d_q"], QT)
                dma("scalar", dbg_out["d_k"], KTB[0])
                dma("scalar", dbg_out["d_v"], VTB[0])
            emit_attn(n)
            if n + 1 < NCH:
                emit_ln(n + 1)
        emit_oproj(NCH - 1)

    if "P3" in phases:
        if "P2" in phases:
            P.wait("scalar", s_t, s_t.n)
            P.wait("scalar", s_wqs, s_wqs.n)
            c_wqrl = dma("scalar", WQ.rearrange("p (k o) -> p k o", o=E),
                         wq_scr.rearrange("(k p) o -> p k o", p=128), sem=s_wqr)
            st8["xst_wait"] = 0
            st8["wq_reload"] = c_wqrl
        p3_src = xB_scr if "P2" in phases else x_in
        ffn_stage(p3_src, y_out, with_wtr=False)

    P.wait("scalar", s_dc, s_dc.n)
    P.wait("sync", s_dc, s_dc.n)
    P.wait("sync", s_dg, s_dg.n)
    P.wait("sync", s_ds, s_ds.n)
    for _sm in (s_ln1, s_wld, s_wqs, s_wc, s_wsa, s_xl, s_xs, s_xln, s_wl, s_xop, s_xpr, s_wqr):
        P.wait("scalar", _sm, _sm.n)
        P.wait("sync", _sm, _sm.n)

    with nc.Block() as blk:
        @blk.sync
        def _(eng):
            for f in P.ops["sync"]:
                f(eng)

        @blk.scalar
        def _(eng):
            for f in P.ops["scalar"]:
                f(eng)

        @blk.vector
        def _(eng):
            for f in P.ops["vector"]:
                f(eng)

        @blk.gpsimd
        def _(eng):
            for f in P.ops["gpsimd"]:
                f(eng)

        @blk.tensor
        def _(eng):
            for f in P.ops["tensor"]:
                f(eng)

    return nc


def kernel(L0_attn_x, L0_attn_w_qkv, L0_attn_w_out, L0_ffn_W, L0_ffn_ln_w, L0_ffn_ln_b):
    nc = build_nc()
    x = np.ascontiguousarray(np.asarray(L0_attn_x, dtype=np.float32))
    wqkv = np.ascontiguousarray(np.asarray(L0_attn_w_qkv, dtype=np.float32))
    wout = np.ascontiguousarray(np.asarray(L0_attn_w_out, dtype=np.float32))
    Wf = np.ascontiguousarray(np.asarray(L0_ffn_W, dtype=np.float32))
    lnw = np.ascontiguousarray(np.asarray(L0_ffn_ln_w, dtype=np.float32)).reshape(1, E)
    lnb = np.ascontiguousarray(np.asarray(L0_ffn_ln_b, dtype=np.float32)).reshape(1, E)
    in_maps = []
    for b in range(8):
        in_maps.append({
            "x": np.ascontiguousarray(x[b]),
            "w_qkv": wqkv,
            "w_out": wout,
            "W_ffn": Wf,
            "ln_w": lnw,
            "ln_b": lnb,
        })
    res = run_bass_kernel_spmd(nc, in_maps, core_ids=list(range(8)))
    return np.stack([np.asarray(r["y"], dtype=np.float32) for r in res.results], axis=0)


# version 26
# speedup vs baseline: 1.1871x; 1.1871x over previous
import numpy as np
import concourse.bass as bass
import concourse.mybir as mybir
from concourse.bass_utils import run_bass_kernel_spmd

F32 = mybir.dt.float32
BF16 = mybir.dt.bfloat16
AF = mybir.ActivationFunctionType
ALU = mybir.AluOpType
AX = mybir.AxisListType

S = 2048
E = 2048
H = 16
D = 128
C = 512
NT = 16
NCH = 4
TPC = 4
EPS = 1e-5
NEG = -1e30
NWO = 64
SCALE = 1.0 / float(np.sqrt(D))


class Sem:
    def __init__(self, nc, name):
        self.h = nc.alloc_semaphore(name)
        self.n = 0


class Prog:
    ENGINES = ("sync", "scalar", "vector", "gpsimd", "tensor")

    def __init__(self):
        self.ops = {e: [] for e in self.ENGINES}

    def emit(self, eng, fn):
        self.ops[eng].append(fn)

    def wait(self, eng, sem, val):
        if val is not None and val > 0:
            self.ops[eng].append(lambda e, s=sem.h, v=val: e.wait_ge(s, v))

    def drain(self, eng):
        self.ops[eng].append(lambda e: e.drain())


def build_nc(phases=("P1", "P2", "P3"), dbg=False):
    phases = set(phases)
    nc = bass.Bass("TRN2", target_bir_lowering=False)
    dbg_out = {}
    def dbg_t(name, shape, dtype):
        dbg_out[name] = nc.dram_tensor(name, shape, dtype, kind="ExternalOutput").ap()
        return dbg_out[name]

    x_in = nc.dram_tensor("x", [S, E], F32, kind="ExternalInput").ap()
    w_qkv = nc.dram_tensor("w_qkv", [3 * E, E], F32, kind="ExternalInput").ap()
    w_out = nc.dram_tensor("w_out", [E, E], F32, kind="ExternalInput").ap()
    W_ffn = nc.dram_tensor("W_ffn", [E, E], F32, kind="ExternalInput").ap()
    lnw_h = nc.dram_tensor("ln_w", [1, E], F32, kind="ExternalInput").ap()
    lnb_h = nc.dram_tensor("ln_b", [1, E], F32, kind="ExternalInput").ap()
    y_out = nc.dram_tensor("y", [S, E], F32, kind="ExternalOutput").ap()

    wT_scr = nc.dram_tensor("wT_scr", [16, 128, 16, 512], BF16, kind="Internal").ap()
    wq_scr = nc.dram_tensor("wq_scr", [E, E], BF16, kind="Internal").ap()
    xA_scr = nc.dram_tensor("xA_scr", [S, E], F32, kind="Internal").ap()
    xB_scr = nc.dram_tensor("xB_scr", [S, E], F32, kind="Internal").ap()

    ar1 = nc.alloc_sbuf_tensor("AR1", [128, 32768], F32).ap()
    ar2 = nc.alloc_sbuf_tensor("AR2", [128, 32768], BF16).ap()
    ar1b = ar1.bitcast(BF16)

    ident = nc.alloc_sbuf_tensor("ident", [128, 128], BF16).ap()
    identf = nc.alloc_sbuf_tensor("identf", [128, 128], F32).ap()
    maskU = nc.alloc_sbuf_tensor("maskU", [128, 128], F32).ap()
    maskL = nc.alloc_sbuf_tensor("maskL", [128, 128], F32).ap()
    zeros = nc.alloc_sbuf_tensor("zeros", [128, 128], F32).ap()
    onesq = nc.alloc_sbuf_tensor("onesq", [128, 128], F32).ap()
    onesp0 = nc.alloc_sbuf_tensor("onesp0", [1, 128], F32).ap()
    onescol = nc.alloc_sbuf_tensor("onescol", [128, 1], F32).ap()
    c_eps = nc.alloc_sbuf_tensor("c_eps", [128, 1], F32).ap()

    scpart = nc.alloc_sbuf_tensor("scpart", [128, 64], F32).ap()
    ln16t = nc.alloc_sbuf_tensor("ln16t", [16, 256], F32).ap()
    lnwcol = nc.alloc_sbuf_tensor("lnwcol", [128, 16], F32).ap()
    lnbcol = nc.alloc_sbuf_tensor("lnbcol", [128, 16], F32).ap()
    onesb = nc.alloc_sbuf_tensor("onesb", [1, 128], BF16).ap()
    c_eps2 = nc.alloc_sbuf_tensor("c_eps2", [128, 1], F32).ap()
    sccol = nc.alloc_sbuf_tensor("sccol", [128, 1], F32).ap()
    s11 = nc.alloc_sbuf_tensor("s11", [1, 1], F32).ap()
    thr = nc.alloc_sbuf_tensor("thr", [128, 1], F32).ap()
    nthr = nc.alloc_sbuf_tensor("nthr", [128, 1], F32).ap()
    tmpcmp = nc.alloc_sbuf_tensor("tmpcmp", [128, 2048], F32).ap()
    biasrow = tmpcmp[0:1, :].bitcast(BF16)[:, 0:2048]

    stats = [nc.alloc_sbuf_tensor(f"stats{i}", [128, 24], F32).ap() for i in range(2)]
    mv1 = [nc.alloc_sbuf_tensor(f"mv1_{i}", [128, 2], F32).ap() for i in range(2)]
    mv2 = [nc.alloc_sbuf_tensor(f"mv2_{i}", [128, 2], F32).ap() for i in range(2)]
    r1 = [nc.alloc_sbuf_tensor(f"r1_{i}", [128, 1], F32).ap() for i in range(2)]
    r2 = [nc.alloc_sbuf_tensor(f"r2_{i}", [128, 1], F32).ap() for i in range(2)]
    lntmp = [nc.alloc_sbuf_tensor(f"lntmp{i}", [128, 1], F32).ap() for i in range(2)]
    negmax = [nc.alloc_sbuf_tensor(f"negmax{i}", [128, 1], F32).ap() for i in range(2)]
    den = [nc.alloc_sbuf_tensor(f"den{i}", [128, 1], F32).ap() for i in range(2)]
    rden = [nc.alloc_sbuf_tensor(f"rden{i}", [128, 1], F32).ap() for i in range(2)]

    mm = [nc.alloc_psum_tensor(f"mm{i}", [128, 512], F32).ap() for i in range(2)]
    sc = [nc.alloc_psum_tensor(f"sc{i}", [128, 1024], F32).ap() for i in range(2)]
    scb = [t.bitcast(BF16) for t in sc]
    pt = [nc.alloc_psum_tensor(f"pt{i}", [128, 1024], BF16).ap() for i in range(2)]

    XST = [ar1[:, i * E:(i + 1) * E] for i in range(3)]
    LNW = ar1[:, 4096:4096 + E]
    LNB = ar1[:, 4096 + E:8192]
    HT = [ar1[:, 8192 + i * E:8192 + (i + 1) * E] for i in range(2)]
    T2 = [ar1[:, 12288 + i * E:12288 + (i + 1) * E] for i in range(2)]
    lnrow = T2[0][0:1, :]
    lnbrow = T2[1][0:1, :]
    T3 = [ar1[:, 16384 + i * E:16384 + (i + 1) * E] for i in range(2)]
    XN2 = [ar1b[:, 40960 + i * E:40960 + (i + 1) * E] for i in range(2)]
    XNT1 = [ar1b[:, 45056 + i * E:45056 + (i + 1) * E] for i in range(2)]
    WNAT = [ar1[:, 24576 + i * E:24576 + (i + 1) * E] for i in range(4)]
    WNATB = [ar1b[:, 16384 + i * E:16384 + (i + 1) * E] for i in range(4)]
    WTR = [ar1b[:, 12288 + i * E:12288 + (i + 1) * E] for i in range(2)]
    WACC = [ar1b[:, 24576 + i * 8192:24576 + (i + 1) * 8192] for i in range(2)]
    W_SB = ar1

    XLN = [ar1[:, i * E:(i + 1) * E] for i in range(2)]
    XOP = [ar1[:, 4096 + i * E:4096 + (i + 1) * E] for i in range(2)]
    KTB = [ar1b[:, 16384 + i * 8192:16384 + (i + 1) * 8192] for i in range(2)]
    VTB = [ar1b[:, 32768 + i * 8192:32768 + (i + 1) * 8192] for i in range(2)]
    QT = ar1b[:, 49152:57344]
    XOPS = [ar1[:, 4096 + i * 512:4096 + (i + 1) * 512] for i in range(2)]
    XPRS = [ar1[:, 5120 + i * 512:5120 + (i + 1) * 512] for i in range(2)]
    OTA = [ar1b[:, 57344:61440], ar1b[:, 12288:16384]]
    OTB = ar1b[:, 61440:65536]

    WQ = ar2
    XNTC = ar2[:, 0:8192]
    WSTR = [ar2[:, 8192 + i * 8192:8192 + (i + 1) * 8192] for i in range(2)]
    PRB = [ar2[:, 24576 + i * 640:24576 + (i + 1) * 640] for i in range(3)]
    ATT = [ar2[:, 26496 + i * 640:26496 + (i + 1) * 640] for i in range(2)]
    HB = [ar2[:, 27776 + i * E:27776 + (i + 1) * E] for i in range(2)]

    P = Prog()
    s_dc = Sem(nc, "s_dc")
    s_ds = Sem(nc, "s_ds")
    s_dg = Sem(nc, "s_dg")
    s_t = Sem(nc, "s_t")
    s_v = Sem(nc, "s_v")
    s_s = Sem(nc, "s_s")
    s_g = Sem(nc, "s_g")
    QSEM = {"scalar": s_dc, "sync": s_ds, "gpsimd": s_dg}
    s_ln1 = Sem(nc, "s_ln1"); s_wld = Sem(nc, "s_wld"); s_wqs = Sem(nc, "s_wqs")
    s_wc = Sem(nc, "s_wc");   s_wsa = Sem(nc, "s_wsa"); s_xl = Sem(nc, "s_xl")
    s_xs = Sem(nc, "s_xs");   s_xln = Sem(nc, "s_xln"); s_wl = Sem(nc, "s_wl")
    s_xop = Sem(nc, "s_xop"); s_xpr = Sem(nc, "s_xpr"); s_wqr = Sem(nc, "s_wqr")
    s_wl2 = Sem(nc, "s_wl2")
    ESEM = {"tensor": s_t, "vector": s_v, "scalar": s_s, "gpsimd": s_g}

    def dma(queue, out, in_, sem=None):
        qs = sem if sem is not None else QSEM[queue]
        qs.n += 16
        P.emit(queue, lambda e, o=out, i=in_, s=qs.h: e.dma_start(out=o, in_=i).then_inc(s, 16))
        return qs.n

    def op(eng, fn, inc=False):
        if inc:
            es = ESEM[eng]
            es.n += 1
            P.emit(eng, lambda e, f=fn, s=es.h: f(e).then_inc(s, 1))
            return es.n
        P.emit(eng, fn)
        return None

    guard = {}

    def pguard(name):
        if name in guard:
            sem, cap = guard[name]
            P.wait("tensor", sem, cap)

    def pread(name, eng, cap):
        guard[name] = (ESEM[eng], cap)

    st8 = {}

    op("gpsimd", lambda e: e.memset(zeros, 0.0))
    op("gpsimd", lambda e: e.memset(onesq, 1.0))
    op("gpsimd", lambda e: e.memset(onesp0, 1.0))
    op("gpsimd", lambda e: e.memset(onescol, 1.0))
    op("gpsimd", lambda e: e.memset(c_eps, EPS))
    op("gpsimd", lambda e: e.memset(c_eps2, EPS * EPS))
    op("gpsimd", lambda e: e.memset(onesb, 1.0))
    P.drain("gpsimd")
    op("gpsimd", lambda e: e.affine_select(out=ident, in_=onesq, pattern=[[1, 128]],
                                           compare_op=ALU.is_equal, fill=0.0, base=0, channel_multiplier=-1))
    op("gpsimd", lambda e: e.affine_select(out=identf, in_=onesq, pattern=[[1, 128]],
                                           compare_op=ALU.is_equal, fill=0.0, base=0, channel_multiplier=-1))
    op("gpsimd", lambda e: e.affine_select(out=maskU, in_=zeros, pattern=[[1, 128]],
                                           compare_op=ALU.is_ge, fill=NEG, base=0, channel_multiplier=-1))
    c_consts = op("gpsimd", lambda e: e.affine_select(out=maskL, in_=zeros, pattern=[[-1, 128]],
                                                      compare_op=ALU.is_ge, fill=NEG, base=0,
                                                      channel_multiplier=1), inc=True)

    ptf0 = pt[0].bitcast(F32)
    dma("scalar", ln16t[:, 0:128], lnw_h.rearrange("a (t c) -> t (a c)", c=128), sem=s_ln1)
    c_lnr = dma("scalar", ln16t[:, 128:256], lnb_h.rearrange("a (t c) -> t (a c)", c=128), sem=s_ln1)

    c_wld = [dma("scalar", W_SB[:, t * E:(t + 1) * E], W_ffn[128 * t:128 * (t + 1), :], sem=s_wld)
             for t in range(NT)]

    P.wait("tensor", s_ln1, c_lnr)
    P.wait("tensor", s_g, c_consts)
    op("tensor", lambda e: e.transpose(out=ptf0[:, 0:16], in_=ln16t[:, 0:128],
                                       identity=identf[0:16, 0:16]))
    c_lnt = op("tensor", lambda e: e.transpose(out=ptf0[:, 16:32], in_=ln16t[:, 128:256],
                                               identity=identf[0:16, 0:16]), inc=True)
    P.wait("scalar", s_t, c_lnt)
    op("scalar", lambda e: e.copy(out=lnwcol, in_=ptf0[:, 0:16]))
    c_lncols = op("scalar", lambda e: e.copy(out=lnbcol, in_=ptf0[:, 16:32]), inc=True)
    pread("pt0", "scalar", c_lncols)

    for t in range(NT):
        P.wait("vector", s_wld, c_wld[t])
        for j in range(4):
            op("vector", lambda e, t=t, j=j: e.tensor_reduce(
                out=scpart[:, 4 * t + j:4 * t + j + 1],
                in_=W_SB[:, t * E + 512 * j:t * E + 512 * (j + 1)],
                axis=AX.X, op=ALU.add, apply_absolute_value=True))
    P.drain("vector")
    c_scp = op("vector", lambda e: e.tensor_reduce(out=sccol, in_=scpart, axis=AX.X, op=ALU.add), inc=True)
    P.wait("tensor", s_v, c_scp)
    P.wait("tensor", s_g, c_consts)
    c_ps1 = op("tensor", lambda e: e.matmul(out=mm[0][0:1, 0:1], lhsT=sccol, rhs=onescol,
                                            start=True, stop=True), inc=True)
    P.wait("vector", s_t, c_ps1)
    op("vector", lambda e: e.tensor_copy(out=s11, in_=mm[0][0:1, 0:1]))
    P.drain("vector")
    c_s11 = op("vector", lambda e: e.tensor_scalar(out=s11, in0=s11, scalar1=0.5 / float(E * E),
                                                   scalar2=None, op0=ALU.mult, op1=ALU.bypass), inc=True)
    pread("mm0", "vector", c_s11)
    P.wait("tensor", s_v, c_s11)
    c_ps2 = op("tensor", lambda e: e.matmul(out=mm[1][:, 0:1], lhsT=onesp0, rhs=s11,
                                            start=True, stop=True), inc=True)
    P.wait("vector", s_t, c_ps2)
    op("vector", lambda e: e.tensor_copy(out=thr, in_=mm[1][:, 0:1]))
    P.drain("vector")
    c_thr = op("vector", lambda e: e.tensor_scalar(out=nthr, in0=thr, scalar1=-1.0, scalar2=None,
                                                   op0=ALU.mult, op1=ALU.bypass), inc=True)
    pread("mm1", "vector", c_thr)
    P.drain("vector")

    LNBREP = ar1b[:, 12288:14336]
    c_wq_tile = [None] * NT
    c_lnbrep_t = [None] * NT

    def emit_lnbrep(t):
        c_lnbrep_t[t] = op("vector", lambda e, t=t: e.tensor_scalar(
            out=LNBREP[:, t * 128:(t + 1) * 128], in0=onesq, scalar1=lnbcol[:, t:t + 1],
            scalar2=None, op0=ALU.mult, op1=ALU.bypass), inc=True)

    for t in range(NT):
        op("vector", lambda e, t=t: e.tensor_scalar(
            out=tmpcmp, in0=W_SB[:, t * E:(t + 1) * E], scalar1=thr,
            scalar2=None, op0=ALU.is_gt, op1=ALU.bypass))
        op("vector", lambda e, t=t: e.tensor_scalar(
            out=WQ[:, t * E:(t + 1) * E], in0=W_SB[:, t * E:(t + 1) * E], scalar1=nthr,
            scalar2=None, op0=ALU.is_lt, op1=ALU.bypass))
        P.drain("vector")
        c_wq_tile[t] = op("vector", lambda e, t=t: e.tensor_tensor(
            out=WQ[:, t * E:(t + 1) * E], in0=tmpcmp, in1=WQ[:, t * E:(t + 1) * E],
            op=ALU.subtract), inc=True)
        if t == 3:
            P.wait("vector", s_s, c_lncols)
            for tt in range(4):
                emit_lnbrep(tt)
        elif t > 3:
            emit_lnbrep(t)
    st8["xst_wait"] = c_wq_tile[2]

    BB = [mm[0], mm[1], sc[0][:, 0:512], sc[1][:, 0:512]]
    BBN = ["mm0", "mm1", "sc0", "sc1"]
    for nm in BBN:
        pguard(nm)
    c_bias_t = [None] * NT
    for t in range(NT):
        P.wait("tensor", s_v, c_wq_tile[t])
        P.wait("tensor", s_v, c_lnbrep_t[t])
        for oc in range(4):
            cap = op("tensor", lambda e, t=t, oc=oc: e.matmul(
                out=BB[oc], lhsT=LNBREP[:, t * 128:(t + 1) * 128],
                rhs=WQ[:, t * E + oc * 512:t * E + (oc + 1) * 512],
                start=(t == 0), stop=(t == 15)), inc=(oc == 3))
        c_bias_t[t] = cap

    c_wqs_tile = [None] * NT
    for t in range(NT - 1):
        P.wait("scalar", s_t, c_bias_t[t])
        c_wqs_tile[t] = op("scalar", lambda e, t=t: e.activation(
            out=WQ[:, t * E:(t + 1) * E], in_=WQ[:, t * E:(t + 1) * E],
            func=AF.Copy, scale=lnwcol[:, t:t + 1]), inc=True)
    P.wait("scalar", s_t, c_bias_t[NT - 1])
    for oc in range(4):
        ccap = op("scalar", lambda e, oc=oc: e.copy(out=biasrow[:, oc * 512:(oc + 1) * 512],
                                                    in_=BB[oc][0:1, :]), inc=True)
        pread(BBN[oc], "scalar", ccap)
    c_wqs_tile[NT - 1] = op("scalar", lambda e, t=NT - 1: e.activation(
        out=WQ[:, t * E:(t + 1) * E], in_=WQ[:, t * E:(t + 1) * E],
        func=AF.Copy, scale=lnwcol[:, t:t + 1]), inc=True)
    st8["bias_done"] = c_wqs_tile[NT - 1]
    c_wq_done = c_wqs_tile[-1]

    if "P1" not in phases:
        for t in range(NT):
            P.wait("scalar", s_s, c_wqs_tile[t])
            dma("scalar", wq_scr[128 * t:128 * (t + 1), :], WQ[:, t * E:(t + 1) * E], sem=s_wqs)
    if dbg:
        dbg_t("d_wq0", [128, 2048], BF16)
        dbg_t("d_bias", [1, 2048], BF16)
        dbg_t("d_lnw", [128, 16], F32)

        P.wait("scalar", s_s, c_wqs_tile[0])
        dma("scalar", dbg_out["d_wq0"], WQ[:, 0:2048])
        P.wait("scalar", s_s, st8["bias_done"])
        dma("scalar", dbg_out["d_bias"], biasrow)
        dma("scalar", dbg_out["d_lnw"], lnwcol)


    c_wcast = [None] * NWO
    c_wcvt = [None] * NWO
    c_wtr = [None] * NWO
    c_wtre = [None] * NWO
    c_gst = [None] * (NWO // 4)

    def w_src(j):
        if j < 48:
            return w_qkv[128 * j:128 * (j + 1), :]
        return w_out[128 * (j - 48):128 * (j - 47), :]

    def emit_wcast(j):
        if j >= 4:
            assert c_wcvt[j - 4] is not None
            P.wait("sync", s_v, c_wcvt[j - 4])
        else:
            P.wait("sync", s_v, c_wq_tile[-1])
        c_wcast[j] = dma("sync", WNAT[j % 4], w_src(j), sem=s_wc)
        P.wait("vector", s_wc, c_wcast[j])
        if j >= 4:
            P.wait("vector", s_t, c_wtr[j - 4])
        c_wcvt[j] = op("vector", lambda e, j=j: e.tensor_copy(out=WNATB[j % 4], in_=WNAT[j % 4]), inc=True)

    def emit_wtr(j):
        P.wait("tensor", s_v, c_wcvt[j])
        if j >= 2:
            P.wait("tensor", s_s, c_wtre[j - 2])
        else:
            pguard("sc0")
            pguard("sc1")
        cap = None
        for ib in range(16):
            cap = op("tensor", lambda e, j=j, ib=ib: e.transpose(
                out=scb[j % 2][:, ib * 128:(ib + 1) * 128],
                in_=WNATB[j % 4][:, ib * 128:(ib + 1) * 128], identity=ident), inc=(ib == 15))
        c_wtr[j] = cap

    def emit_wtre(j):
        g, u = j // 4, j % 4
        P.wait("scalar", s_t, c_wtr[j])
        if u == 0 and g >= 2:
            P.wait("scalar", s_wsa, c_gst[g - 2])
        dst = WACC[g % 2].rearrange("p (k o) -> p k o", o=512)[:, :, u * 128:(u + 1) * 128]
        src = scb[j % 2].rearrange("p (k o) -> p k o", o=128)
        c_wtre[j] = op("scalar", lambda e, dst=dst, src=src: e.copy(out=dst, in_=src), inc=True)
        pread("sc%d" % (j % 2), "scalar", c_wtre[j])


    def emit_wstore(g):
        P.wait("sync", s_s, c_wtre[4 * g + 3])
        if g < 12:
            dst = wT_scr[g]
        else:
            dst = wT_scr[g]
        c_gst[g] = dma("sync", dst.rearrange("p k o -> p (k o)"), WACC[g % 2], sem=s_wsa)

    done_p1 = [False]

    def ffn_stage(src_hbm, dst_hbm, with_wtr):
        c_load = [None] * NT
        c_stats1 = [None] * NT
        c_n1s2 = [None] * NT
        c_t2 = [None] * NT
        c_xn2 = [None] * NT
        c_transp = [None] * NT
        c_xnt = [None] * NT
        c_rstd1 = [None] * NT
        c_rstd2 = [None] * NT
        c_mmg = [[None] * 4 for _ in range(NT)]
        c_stt = [[None] * 4 for _ in range(NT)]
        c_store = [None] * NT

        for i in range(NT + 2):
            if i < NT:
                if i < 3:
                    P.wait("scalar", s_v, st8["xst_wait"])
                else:
                    P.wait("scalar", s_xs, c_store[i - 3])
                c_load[i] = dma("scalar", XST[i % 3], src_hbm[128 * i:128 * (i + 1), :], sem=s_xl)
            if with_wtr and i < NT:
                P.wait("sync", s_s, c_wqs_tile[i])
                dma("sync", wq_scr[128 * i:128 * (i + 1), :], WQ[:, i * E:(i + 1) * E], sem=s_wqs)
            if i >= 2:
                P.wait("scalar", s_v, c_stt[i - 2][3])
                c_store[i - 2] = dma("scalar", dst_hbm[128 * (i - 2):128 * (i - 1), :],
                                     XST[(i - 2) % 3], sem=s_xs)

            if 1 <= i <= NT:
                ii = i - 1
                P.wait("vector", s_t, c_transp[ii])
                op("vector", lambda e, ii=ii: e.tensor_copy(out=XNT1[ii % 2][:, 0:1024], in_=pt[0]))
                c_xnt[ii] = op("vector", lambda e, ii=ii: e.tensor_copy(out=XNT1[ii % 2][:, 1024:2048],
                                                                        in_=pt[1]), inc=True)
                pread("pt0", "vector", c_xnt[ii])
                pread("pt1", "vector", c_xnt[ii])

            if i < NT:
                P.wait("vector", s_xl, c_load[i])

                def v_stats1(e, i=i):
                    for j in range(4):
                        e.bn_stats(stats[i % 2][:, 6 * j:6 * (j + 1)], XST[i % 3][:, 512 * j:512 * (j + 1)])
                    e.drain()
                    return e.bn_aggr(mv1[i % 2], stats[i % 2])
                c_stats1[i] = op("vector", v_stats1, inc=True)

                P.wait("scalar", s_v, c_stats1[i])
                op("scalar", lambda e, i=i: e.activation(out=lntmp[i % 2], in_=mv1[i % 2][:, 1:2],
                                                         func=AF.Ln, scale=1.0 + EPS, bias=c_eps2[:, 0:1]))
                P.drain("scalar")
                c_rstd1[i] = op("scalar", lambda e, i=i: e.activation(out=r1[i % 2], in_=lntmp[i % 2],
                                                                     func=AF.Exp, scale=-0.5), inc=True)

                P.wait("vector", s_s, c_rstd1[i])
                if i >= 2:
                    P.wait("vector", s_t, c_transp[i - 2])
                c_xn2[i] = op("vector", lambda e, i=i: e.tensor_scalar(
                    out=XN2[i % 2], in0=XST[i % 3], scalar1=mv1[i % 2][:, 0:1],
                    scalar2=r1[i % 2], op0=ALU.subtract, op1=ALU.mult), inc=True)

                if with_wtr and i >= 1:
                    for j in range(4 * (i - 1), 4 * (i - 1) + 4):
                        emit_wcast(j)


            if with_wtr and 0 <= i - 2 < NT:
                emit_wstore(i - 2)

            if 1 <= i <= NT:
                ii = i - 1
                P.wait("tensor", s_v, c_xnt[ii])
                if ii == 0:
                    P.wait("tensor", s_s, st8.get("bias_done", 0))
                    P.wait("tensor", s_wqr, st8.get("wq_reload", 0))
                if with_wtr and ii == NT - 1:
                    for j in range(4 * ii, 4 * ii + 4):
                        emit_wcast(j)
                def emit_stt(ii, oc):
                    if oc == 0 and ii >= 3:
                        P.wait("vector", s_xs, c_store[ii - 3])
                    P.wait("vector", s_t, c_mmg[ii][oc])
                    c_stt[ii][oc] = op("vector", lambda e, ii=ii, oc=oc: e.scalar_tensor_tensor(
                        out=XST[ii % 3][:, oc * 512:(oc + 1) * 512], in0=mm[oc % 2], scalar=0.5,
                        in1=XST[ii % 3][:, oc * 512:(oc + 1) * 512], op0=ALU.mult, op1=ALU.add), inc=True)
                    pread("mm%d" % (oc % 2), "vector", c_stt[ii][oc])

                for oc in range(4):
                    if with_wtr:
                        emit_wtr(4 * ii + oc)
                        emit_wtre(4 * ii + oc)
                    if oc >= 2:
                        emit_stt(ii, oc - 2)
                    pguard("mm%d" % (oc % 2))
                    op("tensor", lambda e, oc=oc: e.matmul(
                        out=mm[oc % 2], lhsT=onesb, rhs=biasrow[:, oc * 512:(oc + 1) * 512],
                        start=True, stop=False))
                    cap = None
                    for k in range(16):
                        cap = op("tensor", lambda e, ii=ii, oc=oc, k=k: e.matmul(
                            out=mm[oc % 2], lhsT=XNT1[ii % 2][:, k * 128:(k + 1) * 128],
                            rhs=WQ[:, k * E + oc * 512:k * E + (oc + 1) * 512],
                            start=False, stop=(k == 15)), inc=(k == 15))
                    c_mmg[ii][oc] = cap
                emit_stt(ii, 2)
                emit_stt(ii, 3)

            if i < NT:
                    P.wait("tensor", s_v, c_xn2[i])
                    pguard("pt0")
                    pguard("pt1")
                    cap = None
                    for ib in range(16):
                        cap = op("tensor", lambda e, i=i, ib=ib: e.transpose(
                            out=pt[ib // 8][:, (ib % 8) * 128:(ib % 8 + 1) * 128],
                            in_=XN2[i % 2][:, ib * 128:(ib + 1) * 128], identity=ident), inc=(ib == 15))
                    c_transp[i] = cap


        done_p1[0] = True
        return {"store": c_store, "last_mm": c_mmg[NT - 1][3]}

    p1_meta = None
    if "P1" in phases:
        p1_dst = xA_scr if "P2" in phases else y_out
        p1_meta = ffn_stage(x_in, p1_dst, with_wtr=True)
    elif "P2" in phases:
        for j in range(NWO):
            emit_wcast(j)
            emit_wtr(j)
            emit_wtre(j)
            if j % 4 == 3:
                emit_wstore(j // 4)

    if "P2" in phases:
        p2_src = xA_scr if "P1" in phases else x_in
        p2_dst = xB_scr if "P3" in phases else y_out

        NTI = NCH * TPC
        c_xln = [None] * NTI
        c_h = [None] * NTI
        c_ltr = [None] * NTI
        c_xntc = [None] * NTI
        c_qkvg = [None] * (NCH * 48)
        c_qkve = [None] * (NCH * 48)
        NG = NCH * H * 4
        c_scores = [None] * NG
        c_negmax = [None] * NG
        c_mask = [None] * NG
        c_recip = [None] * NG
        c_exp = [None] * NG
        c_norm = [None] * NG
        c_trb = [None] * NG
        c_atte = [None] * NG
        c_pv = [None] * NG
        c_ote = [None] * (NCH * H)
        c_opg = [[None] * 16 for _ in range(NCH)]
        c_stt2 = [None] * NTI
        c_stt2f = [None] * (NCH * 16)
        c_xopf = [None] * (NCH * 16)
        c_xprf = [None] * (NCH * 16)

        def seg_list(n, t):
            if n == 0:
                return [(0, 0, 128 * (t + 1), 0)], 128 * (t + 1)
            segs = [(-1, 128 * t, 512 - 128 * t, 0)]
            if t > 0:
                segs.append((0, 0, 128 * t, 512 - 128 * t))
            segs.append((0, 128 * t, 128, 512))
            return segs, 640

        def pv_tiles(n, t):
            if n == 0:
                return [(0, j, 128 * j) for j in range(t + 1)]
            return [((-1 if t + j < 4 else 0), (t + j) % 4, 128 * j) for j in range(5)]

        def emit_ln(m):
            for st in range(TPC):
                gi = m * TPC + st
                if gi >= 2:
                    P.wait("scalar", s_v, c_h[gi - 2])
                c_xln[gi] = dma("scalar", XLN[gi % 2], p2_src[128 * gi:128 * (gi + 1), :], sem=s_xln)
                P.wait("vector", s_xln, c_xln[gi])

                def v_stats(e, gi=gi):
                    for j in range(4):
                        e.bn_stats(stats[gi % 2][:, 6 * j:6 * (j + 1)], XLN[gi % 2][:, 512 * j:512 * (j + 1)])
                    e.drain()
                    return e.bn_aggr(mv1[gi % 2], stats[gi % 2])
                cs = op("vector", v_stats, inc=True)
                P.wait("scalar", s_v, cs)
                op("scalar", lambda e, gi=gi: e.activation(out=lntmp[gi % 2], in_=mv1[gi % 2][:, 1:2],
                                                           func=AF.Ln, bias=c_eps[:, 0:1]))
                P.drain("scalar")
                cr = op("scalar", lambda e, gi=gi: e.activation(out=r1[gi % 2], in_=lntmp[gi % 2],
                                                                func=AF.Exp, scale=-0.5), inc=True)
                P.wait("vector", s_s, cr)
                if gi >= 2:
                    P.wait("vector", s_t, c_ltr[gi - 2])
                c_h[gi] = op("vector", lambda e, gi=gi: e.tensor_scalar(
                    out=HB[gi % 2], in0=XLN[gi % 2], scalar1=mv1[gi % 2][:, 0:1],
                    scalar2=r1[gi % 2], op0=ALU.subtract, op1=ALU.mult), inc=True)
                P.wait("tensor", s_v, c_h[gi])
                pguard("pt0")
                pguard("pt1")
                cap = None
                for ib in range(16):
                    cap = op("tensor", lambda e, gi=gi, ib=ib: e.transpose(
                        out=pt[ib // 8][:, (ib % 8) * 128:(ib % 8 + 1) * 128],
                        in_=HB[gi % 2][:, ib * 128:(ib + 1) * 128], identity=ident), inc=(ib == 15))
                c_ltr[gi] = cap
                P.wait("vector", s_t, c_ltr[gi])
                xv = XNTC.rearrange("p (k s) -> p k s", s=512)
                op("vector", lambda e, st=st, xv=xv: e.tensor_copy(
                    out=xv[:, 0:8, st * 128:(st + 1) * 128],
                    in_=pt[0].rearrange("p (k s) -> p k s", s=128)))
                c_xntc[gi] = op("vector", lambda e, st=st, xv=xv: e.tensor_copy(
                    out=xv[:, 8:16, st * 128:(st + 1) * 128],
                    in_=pt[1].rearrange("p (k s) -> p k s", s=128)), inc=True)
                pread("pt0", "vector", c_xntc[gi])
                pread("pt1", "vector", c_xntc[gi])

        wseq = {"pos": 0}
        c_wlp = [None] * (NCH * 16)
        wuse_p = [None] * (NCH * 16)

        def wl_sem(pos):
            return s_wl if pos % 2 == 0 else s_wl2

        def wait_wl(eng, u):
            P.wait(eng, wl_sem(u), c_wlp[u])

        def emit_wload(j):
            pos = wseq["pos"]
            wseq["pos"] += 1
            q = "sync" if pos % 2 == 0 else "gpsimd"
            if pos >= 2:
                assert wuse_p[pos - 2] is not None, f"wload pos {pos} before consumer of {pos-2}"
                P.wait(q, s_t, wuse_p[pos - 2])
            else:
                P.wait(q, s_wsa, c_gst[NWO // 4 - 1])
                if p1_meta is not None:
                    P.wait(q, s_t, p1_meta["last_mm"])
            src = wT_scr[j].rearrange("p k o -> p (k o)")
            c_wlp[pos] = dma(q, WSTR[pos % 2], src, sem=wl_sem(pos))
            return pos

        def emit_qkv(n, qpos):
            for oc in range(12):
                u = qpos[oc]
                for grp in range(4):
                    g_idx = n * 48 + oc * 4 + grp
                    wait_wl("tensor", u)
                    if grp == 0 and oc == 0:
                        P.wait("tensor", s_v, c_xntc[n * TPC + 3])
                    pguard("mm%d" % (g_idx % 2))
                    bank = mm[g_idx % 2]
                    cap = None
                    if oc < 8:
                        for k in range(16):
                            cap = op("tensor", lambda e, u=u, grp=grp, k=k, bank=bank: e.matmul(
                                out=bank, lhsT=WSTR[u % 2][:, k * 512 + grp * 128:k * 512 + (grp + 1) * 128],
                                rhs=XNTC[:, k * 512:(k + 1) * 512],
                                start=(k == 0), stop=(k == 15)), inc=(k == 15))
                    else:
                        for k in range(16):
                            cap = op("tensor", lambda e, u=u, grp=grp, k=k, bank=bank: e.matmul(
                                out=bank, lhsT=XNTC[:, k * 512 + grp * 128:k * 512 + (grp + 1) * 128],
                                rhs=WSTR[u % 2][:, k * 512:(k + 1) * 512],
                                start=(k == 0), stop=(k == 15)), inc=(k == 15))
                    c_qkvg[g_idx] = cap
                    wuse_p[u] = cap
                    P.wait("scalar", s_t, cap)
                    if oc < 4:
                        h = oc * 4 + grp
                        dst = QT[:, h * 512:(h + 1) * 512]
                        c_qkve[g_idx] = op("scalar", lambda e, dst=dst, bank=bank: e.mul(
                            out=dst, in_=bank, mul=SCALE), inc=True)
                    else:
                        if oc < 8:
                            h = (oc - 4) * 4 + grp
                            dst = KTB[n % 2][:, h * 512:(h + 1) * 512]
                        else:
                            dst = VTB[n % 2][:, grp * 2048 + (oc - 8) * 512:grp * 2048 + (oc - 7) * 512]
                        c_qkve[g_idx] = op("scalar", lambda e, dst=dst, bank=bank: e.copy(
                            out=dst, in_=bank), inc=True)
                    pread("mm%d" % (g_idx % 2), "scalar", c_qkve[g_idx])
                if oc + 2 <= 11:
                    qpos.append(emit_wload(oc + 2))

        def emit_attn(n):
            g0 = n * H * 4
            blocks = [(h, t) for h in range(H) for t in range(4)]

            def width_of(t):
                return 128 * (t + 1) if n == 0 else 640

            def emit_scores(h, t, g):
                segs, width = seg_list(n, t)
                pguard("sc%d" % (g % 2))
                if h == 0 and t == 0:
                    P.wait("tensor", s_s, c_qkve[n * 48 + 47])
                cap = None
                for si, (pd, soff, w, doff) in enumerate(segs):
                    par = (n + pd) % 2
                    cap = op("tensor", lambda e, h=h, t=t, g=g, par=par, soff=soff, w=w, doff=doff: e.matmul(
                        out=sc[g % 2][:, doff:doff + w],
                        lhsT=QT[:, h * 512 + t * 128:h * 512 + (t + 1) * 128],
                        rhs=KTB[par][:, h * 512 + soff:h * 512 + soff + w],
                        start=True, stop=True), inc=(si == len(segs) - 1))
                c_scores[g] = cap

            def emit_vblock(h, t, g):
                width = width_of(t)
                P.wait("vector", s_t, c_scores[g])
                c_negmax[g] = op("vector", lambda e, g=g, width=width: e.tensor_reduce(
                    out=negmax[g % 2], in_=sc[g % 2][:, 0:width], axis=AX.X, op=ALU.max,
                    negate=True), inc=True)
                if n == 0:
                    c_mask[g] = op("vector", lambda e, t=t, g=g: e.tensor_tensor(
                        out=sc[g % 2][:, 128 * t:128 * (t + 1)], in0=sc[g % 2][:, 128 * t:128 * (t + 1)],
                        in1=maskL, op=ALU.add), inc=True)
                else:
                    op("vector", lambda e, g=g: e.tensor_tensor(
                        out=sc[g % 2][:, 0:128], in0=sc[g % 2][:, 0:128], in1=maskU, op=ALU.add))
                    c_mask[g] = op("vector", lambda e, g=g: e.tensor_tensor(
                        out=sc[g % 2][:, 512:640], in0=sc[g % 2][:, 512:640], in1=maskL, op=ALU.add), inc=True)

            def emit_exp(h, t, g):
                width = width_of(t)
                P.wait("scalar", s_v, c_mask[g])
                if g - 3 >= 0 and c_trb[g - 3] is not None:
                    P.wait("scalar", s_t, c_trb[g - 3])
                c_exp[g] = op("scalar", lambda e, g=g, width=width: e.activation(
                    out=PRB[g % 3][:, 0:width], in_=sc[g % 2][:, 0:width], func=AF.Exp,
                    bias=negmax[g % 2], scale=1.0, accum_out=den[g % 2]), inc=True)
                pread("sc%d" % (g % 2), "scalar", c_exp[g])

            def emit_recip(g):
                P.wait("vector", s_s, c_exp[g])
                c_recip[g] = op("vector", lambda e, g=g: e.reciprocal(rden[g % 2], den[g % 2]), inc=True)

            def emit_norm(g, t):
                width = width_of(t)
                P.wait("scalar", s_v, c_recip[g])
                c_norm[g] = op("scalar", lambda e, g=g, width=width: e.activation(
                    out=PRB[g % 3][:, 0:width], in_=PRB[g % 3][:, 0:width], func=AF.Copy,
                    scale=rden[g % 2]), inc=True)

            def emit_transp(g, t):
                width = width_of(t)
                P.wait("tensor", s_s, c_norm[g])
                pguard("pt%d" % (g % 2))
                nt_ = width // 128
                cap = None
                for j in range(nt_):
                    cap = op("tensor", lambda e, g=g, j=j: e.transpose(
                        out=pt[g % 2][:, j * 128:(j + 1) * 128],
                        in_=PRB[g % 3][:, j * 128:(j + 1) * 128], identity=ident), inc=(j == nt_ - 1))
                c_trb[g] = cap

            def emit_atte(g, t):
                width = width_of(t)
                P.wait("vector", s_t, c_trb[g])
                if g - 2 >= 0 and c_pv[g - 2] is not None:
                    P.wait("vector", s_t, c_pv[g - 2])
                c_atte[g] = op("vector", lambda e, g=g, width=width: e.tensor_copy(
                    out=ATT[g % 2][:, 0:width], in_=pt[g % 2][:, 0:width]), inc=True)
                pread("pt%d" % (g % 2), "vector", c_atte[g])

            def emit_pv(h, t, g):
                P.wait("tensor", s_v, c_atte[g])
                if t == 0:
                    pguard("mm%d" % (h % 2))
                tiles = pv_tiles(n, t)
                cap = None
                for j, (pd, stile, aoff) in enumerate(tiles):
                    cap = op("tensor", lambda e, h=h, t=t, g=g, pd=pd, stile=stile, aoff=aoff,
                             j=j, lastj=len(tiles) - 1: e.matmul(
                        out=mm[h % 2][:, t * 128:(t + 1) * 128],
                        lhsT=VTB[(n + pd) % 2][:, stile * 2048 + h * 128:stile * 2048 + (h + 1) * 128],
                        rhs=ATT[g % 2][:, aoff:aoff + 128],
                        start=(j == 0), stop=(j == lastj)), inc=(j == len(tiles) - 1))
                c_pv[g] = cap
                if t == 3:
                    hh = n * H + h
                    P.wait("scalar", s_t, c_pv[g])
                    if h < 8:
                        dst = OTA[n % 2][:, h * 512:(h + 1) * 512]
                    else:
                        dst = OTB[:, (h - 8) * 512:(h - 7) * 512]
                        if n >= 1 and h == 8:
                            P.wait("scalar", s_t, c_opg[n - 1][15])
                    c_ote[hh] = op("scalar", lambda e, dst=dst, h=h: e.copy(
                        out=dst, in_=mm[h % 2]), inc=True)
                    pread("mm%d" % (h % 2), "scalar", c_ote[hh])

            NB = len(blocks)
            m = n - 1
            op_state = {"next": 0, "upos": None, "nxt": []}
            if m >= 0:
                op_state["upos"] = emit_wload(12)
                op_state["nxt"].append(emit_wload(13))

            def inject_opgroups(k):
                for _ in range(k):
                    gq2 = op_state["next"]
                    if gq2 >= 16:
                        return
                    if gq2 % 4 == 0 and gq2 > 0:
                        op_state["upos"] = op_state["nxt"].pop(0)
                        if 12 + gq2 // 4 + 1 <= 15:
                            op_state["nxt"].append(emit_wload(12 + gq2 // 4 + 1))
                    emit_opgroup(m, gq2, op_state["upos"])
                    op_state["next"] = gq2 + 1

            for bi in range(NB + 3):
                if bi < NB:
                    h, t = blocks[bi]
                    emit_scores(h, t, g0 + bi)
                    emit_vblock(h, t, g0 + bi)
                    emit_exp(h, t, g0 + bi)
                if 0 <= bi - 1 < NB:
                    emit_recip(g0 + bi - 1)
                    emit_norm(g0 + bi - 1, blocks[bi - 1][1])
                if 0 <= bi - 2 < NB:
                    emit_transp(g0 + bi - 2, blocks[bi - 2][1])
                    emit_atte(g0 + bi - 2, blocks[bi - 2][1])
                if 0 <= bi - 3 < NB:
                    emit_pv(blocks[bi - 3][0], blocks[bi - 3][1], g0 + bi - 3)
                    if m >= 0 and blocks[bi - 3][1] == 3:
                        inject_opgroups(2)
            if m >= 0:
                inject_opgroups(16)

        def ot_lhsT(m, k, st):
            if k < 8:
                return OTA[m % 2][:, k * 512 + st * 128:k * 512 + (st + 1) * 128]
            return OTB[:, (k - 8) * 512 + st * 128:(k - 8) * 512 + (st + 1) * 128]

        def emit_opgroup(m, gq2, u):
            ec, st = gq2 // 4, gq2 % 4
            gg = m * 16 + gq2
            gi = m * TPC + st
            if gg >= 2:
                P.wait("scalar", s_v, c_stt2f[gg - 2])
            c_xopf[gg] = dma("scalar", XOPS[gq2 % 2],
                             p2_src[128 * gi:128 * (gi + 1), ec * 512:(ec + 1) * 512], sem=s_xop)
            wait_wl("tensor", u)
            pguard("mm%d" % (gq2 % 2))
            if gq2 == 0:
                P.wait("tensor", s_s, c_ote[m * H + 15])
            bank = mm[gq2 % 2]
            cap = None
            for k in range(16):
                cap = op("tensor", lambda e, m=m, u=u, st=st, k=k, bank=bank: e.matmul(
                    out=bank, lhsT=ot_lhsT(m, k, st),
                    rhs=WSTR[u % 2][:, k * 512:(k + 1) * 512],
                    start=(k == 0), stop=(k == 15)), inc=(k == 15))
            c_opg[m][gq2] = cap
            wuse_p[u] = cap
            P.wait("vector", s_t, cap)
            P.wait("vector", s_xop, c_xopf[gg])
            if gg >= 2:
                P.wait("vector", s_xpr, c_xprf[gg - 2])
            cstt = op("vector", lambda e, gq2=gq2, bank=bank: e.tensor_tensor(
                out=XPRS[gq2 % 2], in0=bank, in1=XOPS[gq2 % 2], op=ALU.add), inc=True)
            pread("mm%d" % (gq2 % 2), "vector", cstt)
            c_stt2f[gg] = cstt
            if ec == 3:
                c_stt2[gi] = cstt
            P.wait("scalar", s_v, cstt)
            c_xprf[gg] = dma("scalar",
                             p2_dst[128 * gi:128 * (gi + 1), ec * 512:(ec + 1) * 512],
                             XPRS[gq2 % 2], sem=s_xpr)

        def emit_oproj(n):
            upos = emit_wload(12)
            nxt = [emit_wload(13)]
            for gq2 in range(16):
                if gq2 % 4 == 0 and gq2 > 0:
                    upos = nxt.pop(0)
                    if 12 + gq2 // 4 + 1 <= 15:
                        nxt.append(emit_wload(12 + gq2 // 4 + 1))
                emit_opgroup(n, gq2, upos)

        G_DBG = 1
        if dbg:
            dbg_t("d_xnt", [128, 8192], BF16)
            dbg_t("d_q", [128, 8192], BF16)
            dbg_t("d_k", [128, 8192], BF16)
            dbg_t("d_v", [128, 8192], BF16)
            dbg_t("d_sc", [128, 256], F32)
            dbg_t("d_nm", [128, 1], F32)
            dbg_t("d_den", [128, 1], F32)
            dbg_t("d_prb", [128, 256], BF16)
            dbg_t("d_att", [128, 256], BF16)
            dbg_t("d_ot", [128, 8192], BF16)
            dbg_t("d_ws", [128, 8192], BF16)
            dbg_t("d_scr", [256, 512], BF16)

        emit_ln(0)
        for n in range(NCH):
            qpos = [emit_wload(0), emit_wload(1)]
            emit_qkv(n, qpos)
            if dbg and n == 0:
                P.wait("scalar", s_v, c_xntc[3])
                dma("scalar", dbg_out["d_xnt"], XNTC)
                P.wait("scalar", s_s, c_qkve[47])
                dma("scalar", dbg_out["d_q"], QT)
                dma("scalar", dbg_out["d_k"], KTB[0])
                dma("scalar", dbg_out["d_v"], VTB[0])
            emit_attn(n)
            if n + 1 < NCH:
                emit_ln(n + 1)
        emit_oproj(NCH - 1)

    if "P3" in phases:
        if "P2" in phases:
            P.wait("scalar", s_t, s_t.n)
            P.wait("scalar", s_wqs, s_wqs.n)
            c_wqrl = dma("scalar", WQ.rearrange("p (k o) -> p k o", o=E),
                         wq_scr.rearrange("(k p) o -> p k o", p=128), sem=s_wqr)
            st8["xst_wait"] = 0
            st8["wq_reload"] = c_wqrl
        p3_src = xB_scr if "P2" in phases else x_in
        ffn_stage(p3_src, y_out, with_wtr=False)

    P.wait("scalar", s_dc, s_dc.n)
    P.wait("sync", s_dc, s_dc.n)
    P.wait("sync", s_dg, s_dg.n)
    P.wait("sync", s_ds, s_ds.n)
    for _sm in (s_ln1, s_wld, s_wqs, s_wc, s_wsa, s_xl, s_xs, s_xln, s_wl, s_wl2, s_xop, s_xpr, s_wqr):
        P.wait("scalar", _sm, _sm.n)
        P.wait("sync", _sm, _sm.n)

    with nc.Block() as blk:
        @blk.sync
        def _(eng):
            for f in P.ops["sync"]:
                f(eng)

        @blk.scalar
        def _(eng):
            for f in P.ops["scalar"]:
                f(eng)

        @blk.vector
        def _(eng):
            for f in P.ops["vector"]:
                f(eng)

        @blk.gpsimd
        def _(eng):
            for f in P.ops["gpsimd"]:
                f(eng)

        @blk.tensor
        def _(eng):
            for f in P.ops["tensor"]:
                f(eng)

    return nc


def kernel(L0_attn_x, L0_attn_w_qkv, L0_attn_w_out, L0_ffn_W, L0_ffn_ln_w, L0_ffn_ln_b):
    nc = build_nc()
    x = np.ascontiguousarray(np.asarray(L0_attn_x, dtype=np.float32))
    wqkv = np.ascontiguousarray(np.asarray(L0_attn_w_qkv, dtype=np.float32))
    wout = np.ascontiguousarray(np.asarray(L0_attn_w_out, dtype=np.float32))
    Wf = np.ascontiguousarray(np.asarray(L0_ffn_W, dtype=np.float32))
    lnw = np.ascontiguousarray(np.asarray(L0_ffn_ln_w, dtype=np.float32)).reshape(1, E)
    lnb = np.ascontiguousarray(np.asarray(L0_ffn_ln_b, dtype=np.float32)).reshape(1, E)
    in_maps = []
    for b in range(8):
        in_maps.append({
            "x": np.ascontiguousarray(x[b]),
            "w_qkv": wqkv,
            "w_out": wout,
            "W_ffn": Wf,
            "ln_w": lnw,
            "ln_b": lnb,
        })
    res = run_bass_kernel_spmd(nc, in_maps, core_ids=list(range(8)))
    return np.stack([np.asarray(r["y"], dtype=np.float32) for r in res.results], axis=0)
